# version 11 (final)
# baseline (speedup 1.0000x reference)
import numpy as np
import ml_dtypes
from contextlib import ExitStack

import concourse.bass as bass
import concourse.mybir as mybir
from concourse.bass_utils import run_bass_kernel_spmd

F32 = mybir.dt.float32
BF16 = mybir.dt.bfloat16
I32 = mybir.dt.int32
AF = mybir.ActivationFunctionType
ALU = mybir.AluOpType
AX = mybir.AxisListType

NCORE = 8
D = 1024
H = 4096
DC = D // 128
HC = H // 128
TT = 512
TB = 4096
NT = TB // TT
NBK = 4
TP = TB * NBK
EPS = 1e-5
QP = 127.0
MAGIC = 12582912.0
MAGICK = 0x51C0DE
MINR = 1e-6
INV_D = 1.0 / D


class RawScope:

    def __enter__(self):
        return self

    def __exit__(self, *a):
        return False

    def enter_context(self, cm):
        return cm.__enter__()


class EngWrap:

    def __init__(self, eng, val, hi, exempt):
        object.__setattr__(self, "_e", eng)
        object.__setattr__(self, "_v", val)
        object.__setattr__(self, "_h", hi)
        object.__setattr__(self, "_x", exempt)

    def __getattr__(self, name):
        return getattr(self._e, name)

    def wait_ge(self, sem, v):
        num = sem.num
        if num in self._x or not isinstance(v, int):
            return self._e.wait_ge(sem, v)
        a = self._v.get(num, 0) + v
        if a > self._h.get(num, 0):
            self._h[num] = a
        return self._e.wait_ge(sem, a)


class FE:

    def __init__(self, eng, sem):
        self.eng, self.sem, self.n = eng, sem, 0

    def __call__(self, ins):
        self.n += 1
        ins.then_inc(self.sem, 1)
        self.eng.wait_ge(self.sem, self.n)


def recip_newton_f(vec, fe, out, in_, scr):
    fe(vec.reciprocal(out=out, in_=in_))
    fe(vec.tensor_tensor(out=scr, in0=in_, in1=out, op=ALU.mult))
    fe(vec.tensor_scalar(out=scr, in0=scr, scalar1=-1.0, scalar2=2.0,
                         op0=ALU.mult, op1=ALU.add))
    fe(vec.tensor_tensor(out=out, in0=out, in1=scr, op=ALU.mult))


def build_mega():
    nc = bass.Bass("TRN2", target_bir_lowering=False)
    u_t = nc.dram_tensor("u", [D, TP], F32, kind="ExternalInput").ap()
    w1T_t = nc.dram_tensor("w1T", [D, H], F32, kind="ExternalInput").ap()
    w2T_t = nc.dram_tensor("w2T", [H, D], F32, kind="ExternalInput").ap()
    waT_t = nc.dram_tensor("waT", [D, D], F32, kind="ExternalInput").ap()
    wbT_t = nc.dram_tensor("wbT", [D, D], F32, kind="ExternalInput").ap()
    cT_t = nc.dram_tensor("cT", [D, D], F32, kind="ExternalInput").ap()
    b1c_t = nc.dram_tensor("b1c", [128, HC], F32, kind="ExternalInput").ap()
    b2c_t = nc.dram_tensor("b2c", [128, DC], F32, kind="ExternalInput").ap()
    bac_t = nc.dram_tensor("bac", [128, DC], F32, kind="ExternalInput").ap()
    bbc_t = nc.dram_tensor("bbc", [128, DC], F32, kind="ExternalInput").ap()
    Dc_t = nc.dram_tensor("Dc", [128, DC], F32, kind="ExternalInput").ap()
    eye_t = nc.dram_tensor("eye", [128, 128], F32, kind="ExternalInput").ap()
    slot_t = nc.dram_tensor("slot", [1, 2], I32, kind="ExternalInput").ap()
    zhin_t = nc.dram_tensor("zhin", [128, DC], F32, kind="ExternalInput").ap()
    ooff_t = nc.dram_tensor("ooff", [1, 2], I32, kind="ExternalInput").ap()
    xo_t = nc.dram_tensor("xo", [D, TB], F32, kind="ExternalOutput").ap()
    stats1_t = nc.dram_tensor("stats1", [2, TP], F32, kind="Internal").ap()
    stats2_t = nc.dram_tensor("stats2", [2, TP], F32, kind="Internal").ap()
    stats3_t = nc.dram_tensor("stats3", [2, TP], F32, kind="Internal").ap()
    h1_t = nc.dram_tensor("h1d", [NBK * H, TB], F32, kind="Internal").ap()
    x2_t = nc.dram_tensor("x2d", [NBK * D, TB], F32, kind="Internal").ap()
    x3_t = nc.dram_tensor("x3d", [NBK * D, TB], F32, kind="Internal").ap()
    hs0_t = nc.dram_tensor("hs0d", [D, TB], F32, kind="Internal").ap()
    P_t = nc.dram_tensor("Pd", [D, TB], F32, kind="Internal").ap()
    hendd_t = nc.dram_tensor("hendd", [128, DC], F32, kind="Internal").ap()
    x3own_t = nc.dram_tensor("x3own", [D, TB], F32, kind="Internal").ap()
    h1own_t = nc.dram_tensor("h1own", [H, TB], F32, kind="Internal").ap()
    wq1_t = nc.dram_tensor("wq1", [D, H], BF16, kind="Internal").ap()
    wq2_t = nc.dram_tensor("wq2", [H, D], BF16, kind="Internal").ap()
    swd1_t = nc.dram_tensor("swd1", [128, 1], F32, kind="Internal").ap()
    swd2_t = nc.dram_tensor("swd2", [128, 1], F32, kind="Internal").ap()
    mloc_t = [nc.dram_tensor(f"mloc{k}", [1, 1], F32, kind="Internal").ap()
              for k in range(4)]
    mall_t = [nc.dram_tensor(f"mall{k}", [128, 1], F32, kind="Internal").ap()
              for k in range(4)]
    cmd_t = [nc.dram_tensor(f"cmd{k}", [1, 1], F32, kind="Internal").ap()
             for k in range(4)]
    gbar_t = nc.dram_tensor("gbar", [1, 1], F32, kind="Internal").ap()
    shm_t = nc.dram_tensor("shm", [2, 8], I32, kind="Internal",
                           addr_space="Shared").ap()
    flg_t = nc.dram_tensor("flg", [2, 8], I32, kind="Internal",
                           addr_space="Shared").ap()

    gstk = ExitStack()
    with gstk:
        G = gstk.enter_context
        gdv = G(nc.sbuf_tensor("gdv", [1, 1], F32))
        gda = G(nc.sbuf_tensor("gda", [1, 1], F32))
        gdg = G(nc.sbuf_tensor("gdg", [1, 1], F32))
        gds = G(nc.sbuf_tensor("gds", [1, 1], F32))
        mxg = G(nc.sbuf_tensor("mxg", [128, 1], F32))
        gmxg = G(nc.sbuf_tensor("gmxg", [128, 1], F32))
        eyeg = G(nc.sbuf_tensor("eyeg", [128, 128], F32))
        onecg = G(nc.sbuf_tensor("onecg", [128, 1], F32))
        onerg = G(nc.sbuf_tensor("onerg", [1, 128], F32))
        ssg = G(nc.sbuf_tensor("ssg", [1, 2], I32))
        oqg = G(nc.sbuf_tensor("oqg", [1, 2], I32))
        bar = G(nc.semaphore("bar"))
        POOL = [nc.semaphore(f"gp{i}").__enter__() for i in range(17)]
        GPOOL = [nc.semaphore(f"gg{i}").__enter__() for i in range(6)]
        PODED = [nc.semaphore(f"pod{k}").__enter__() for k in range(4)]
        blk = G(nc.Block())

        nbv = [0]
        ids = {}
        SVAL = {}
        SHI = {}
        EXEMPT = {bar.num} | {p.num for p in PODED}

        def EW(eng):
            return EngWrap(eng, SVAL, SHI, EXEMPT)

        def bump(n=19):
            nbv[0] += n
            SVAL.update(SHI)

        with ExitStack() as st:
            S = st.enter_context
            li = POOL[0]
            li2 = GPOOL[0]

            @blk.sync
            def _(sync):
                sync = EW(sync)
                sync.dma_start(out=eyeg[:], in_=eye_t).then_inc(li, 16)

            @blk.gpsimd
            def _(gpsimd):
                gpsimd = EW(gpsimd)
                gpsimd.dma_start(out=ssg[:], in_=slot_t).then_inc(li2, 16)
                gpsimd.dma_start(out=oqg[:], in_=ooff_t).then_inc(li2, 16)
                gpsimd.wait_ge(li2, 32)
                ids["my"] = gpsimd.value_load(ssg[0:1, 0:1])
                ids["offd"] = gpsimd.value_load(oqg[0:1, 0:1])
                ids["offh"] = gpsimd.value_load(oqg[0:1, 1:2])

            @blk.vector
            def _(vector):
                vector = EW(vector)
                vector.wait_ge(li, 16)
                vector.wait_ge(li2, 32)
                ids["voffd"] = vector.value_load(oqg[0:1, 0:1])
                ids["voffh"] = vector.value_load(oqg[0:1, 1:2])
                vector.memset(gds[:], 0.0)
                vector.memset(onecg[:], 1.0)
                vector.memset(onerg[:], 1.0).then_inc(bar, 1)

            @blk.gpsimd
            def _(gpsimd):
                gpsimd = EW(gpsimd)
                gpsimd.wait_ge(bar, 1)
        nbv[0] = 1
        SVAL.update(SHI)

        def emit_stats(tag, src_ap, stats_t, mloc_out):
            NB0 = nbv[0]
            NTT = TP // TT
            with ExitStack() as st:
                S = st.enter_context
                ub = [S(nc.sbuf_tensor(f"{tag}ub{i}", [128, DC * TT], F32))
                      for i in range(2)]
                sqt = [S(nc.sbuf_tensor(f"{tag}sq{i}", [128, TT], F32))
                       for i in range(2)]
                mr = S(nc.sbuf_tensor(f"{tag}mr", [1, TT], F32))
                rr = S(nc.sbuf_tensor(f"{tag}rr", [1, TT], F32))
                t0 = S(nc.sbuf_tensor(f"{tag}t0", [1, TT], F32))
                t1 = S(nc.sbuf_tensor(f"{tag}t1", [1, TT], F32))
                vr = S(nc.sbuf_tensor(f"{tag}vr", [1, TT], F32))
                lnt = S(nc.sbuf_tensor(f"{tag}lnt", [1, TT], F32))
                tmp = S(nc.sbuf_tensor(f"{tag}tmp", [128, DC * TT], F32))
                red8 = S(nc.sbuf_tensor(f"{tag}red8", [128, DC], F32))
                rt = S(nc.sbuf_tensor(f"{tag}rt", [128, 1], F32))
                sc = S(nc.sbuf_tensor(f"{tag}sc", [1, 1], F32))
                psS = [S(nc.psum_tensor(f"{tag}psS{i}", [128, TT], F32))
                       for i in range(2)]
                psM = [S(nc.psum_tensor(f"{tag}psM{i}", [128, TT], F32))
                       for i in range(2)]
                psA = [S(nc.psum_tensor(f"{tag}psA{i}", [128, TT], F32))
                       for i in range(2)]
                psB = [S(nc.psum_tensor(f"{tag}psB{i}", [128, TT], F32))
                       for i in range(2)]
                lu = [POOL[0], POOL[1]]
                srl = POOL[2]
                sv = POOL[3]
                sp = POOL[4]
                sa = POOL[5]
                so = POOL[6]
                vf_s = POOL[7]
                af_s = POOL[8]

                V_INIT = 1
                V_ROW = lambda t: 2 + 3 * t
                V_VAR = lambda t: 3 + 3 * t
                V_MX = lambda t: 4 + 3 * t
                V_FIN = V_MX(NTT - 1)
                V_SC = V_FIN + 1
                A_SQ = lambda t, d: 9 * t + d + 1
                A_RST = lambda t: 9 * t + 9
                P_S1 = lambda t: 10 * t + 1
                P_SQ = lambda t, d: 10 * t + 2 + d
                P_S2 = lambda t: 10 * t + 9
                P_BC = lambda t: 10 * t + 10
                P_FIN = 10 * NTT + 1

                @blk.sync
                def _(sync):
                    sync = EW(sync)
                    sync.wait_ge(bar, NB0)
                    for t in range(NTT):
                        if t >= 2:
                            sync.wait_ge(sv, V_MX(t - 2))
                        sync.dma_start(
                            out=ub[t % 2][:].rearrange("p (c t) -> p c t", c=DC),
                            in_=src_ap(t).rearrange("(c p) t -> p c t", p=128),
                        ).then_inc(lu[t % 2], 16)
                    sync.wait_ge(sv, V_SC)
                    sync.dma_start(out=mloc_out, in_=sc[:]).then_inc(so, 16)
                    sync.wait_ge(so, 16)
                    sync.wait_ge(srl, 16 * 2 * NTT)
                    sync.dma_start(out=gbar_t, in_=gds[:]).then_inc(bar, 16)

                @blk.gpsimd
                def _(gpsimd):
                    gpsimd = EW(gpsimd)
                    gpsimd.wait_ge(bar, NB0)
                    gpsimd.memset(gdg[:], 0.0).then_inc(bar, 1)

                @blk.tensor
                def _(tensor):
                    tensor = EW(tensor)
                    tensor.wait_ge(bar, NB0)
                    for t in range(NTT):
                        tensor.wait_ge(lu[t % 2], 16 * (t // 2 + 1))
                        if t >= 2:
                            tensor.wait_ge(sv, V_MX(t - 2))
                        for d in range(DC):
                            ds = slice(d * TT, (d + 1) * TT)
                            ins = tensor.matmul(out=psS[t % 2][0:1, :],
                                                lhsT=onecg[:],
                                                rhs=ub[t % 2][:, ds],
                                                start=(d == 0),
                                                stop=(d == DC - 1))
                        ins.then_inc(sp, 1)
                        for d in range(DC):
                            tensor.wait_ge(sa, A_SQ(t, d))
                            tensor.matmul(out=psM[t % 2][0:1, :], lhsT=onecg[:],
                                          rhs=sqt[(t * DC + d) % 2][:],
                                          start=(d == 0), stop=(d == DC - 1)
                                          ).then_inc(sp, 1)
                        tensor.wait_ge(sa, A_RST(t))
                        tensor.matmul(out=psA[t % 2][:], lhsT=onerg[:], rhs=mr[:],
                                      start=True, stop=True)
                        tensor.matmul(out=psB[t % 2][:], lhsT=onerg[:], rhs=rr[:],
                                      start=True, stop=True).then_inc(sp, 1)
                    tensor.wait_ge(sv, V_FIN)
                    tensor.transpose(out=psS[0][0:1, 0:128], in_=mxg[:],
                                     identity=eyeg[:]).then_inc(sp, 1)

                @blk.scalar
                def _(scalar):
                    scalar = EW(scalar)
                    scalar.wait_ge(bar, NB0)
                    af = FE(scalar, af_s)
                    for t in range(NTT):
                        ts_ = slice(t * TT, (t + 1) * TT)
                        scalar.wait_ge(lu[t % 2], 16 * (t // 2 + 1))
                        if t >= 1:
                            scalar.wait_ge(srl, 32 * t)
                        for d in range(DC):
                            ds = slice(d * TT, (d + 1) * TT)
                            g = t * DC + d
                            if g >= 2:
                                scalar.wait_ge(sp, P_SQ((g - 2) // DC,
                                                        (g - 2) % DC))
                            scalar.activation(out=sqt[g % 2][:],
                                              in_=ub[t % 2][:, ds],
                                              func=AF.Square).then_inc(sa, 1)
                        scalar.wait_ge(sv, V_VAR(t))
                        if t >= 1:
                            scalar.wait_ge(sa, A_RST(t - 1))
                        af(scalar.activation(out=lnt[:], in_=vr[:], func=AF.Ln))
                        scalar.activation(out=rr[:], in_=lnt[:], func=AF.Exp,
                                          scale=-0.5).then_inc(sa, 1)
                        scalar.wait_ge(sa, A_RST(t))
                        scalar.dma_start(out=stats_t[0:1, ts_], in_=mr[:]
                                         ).then_inc(srl, 16)
                        scalar.dma_start(out=stats_t[1:2, ts_], in_=rr[:]
                                         ).then_inc(srl, 16)
                    scalar.activation(out=gda[:], in_=gds[:],
                                      func=AF.Identity).then_inc(bar, 1)

                @blk.vector
                def _(vector):
                    vector = EW(vector)
                    vector.wait_ge(bar, NB0)
                    vf = FE(vector, vf_s)
                    vector.memset(mxg[:], 0.0).then_inc(sv, 1)
                    vector.wait_ge(sv, V_INIT)
                    for t in range(NTT):
                        vector.wait_ge(sp, P_S1(t))
                        if t >= 1:
                            vector.wait_ge(srl, 32 * t)
                        vf(vector.tensor_scalar(out=mr[:], in0=psS[t % 2][0:1, :],
                                                scalar1=INV_D, scalar2=None,
                                                op0=ALU.mult))
                        vector.wait_ge(sp, P_S2(t))
                        vector.tensor_scalar(out=t0[:], in0=psM[t % 2][0:1, :],
                                             scalar1=INV_D, scalar2=None,
                                             op0=ALU.mult).then_inc(sv, 1)
                        vf(vector.tensor_tensor(out=t1[:], in0=mr[:], in1=mr[:],
                                                op=ALU.mult))
                        vector.scalar_tensor_tensor(out=vr[:], in0=t0[:],
                                                    scalar=EPS, in1=t1[:],
                                                    op0=ALU.add,
                                                    op1=ALU.subtract
                                                    ).then_inc(sv, 1)
                        vector.wait_ge(sv, V_VAR(t))
                        vector.wait_ge(sp, P_BC(t))
                        for d in range(DC):
                            ds = slice(d * TT, (d + 1) * TT)
                            ins = vector.tensor_sub(out=tmp[:, ds],
                                                    in0=ub[t % 2][:, ds],
                                                    in1=psA[t % 2][:])
                        vf(ins)
                        for d in range(DC):
                            ds = slice(d * TT, (d + 1) * TT)
                            ins = vector.tensor_tensor(out=tmp[:, ds],
                                                       in0=tmp[:, ds],
                                                       in1=psB[t % 2][:],
                                                       op=ALU.mult)
                        vf(ins)
                        for d in range(DC):
                            ds = slice(d * TT, (d + 1) * TT)
                            ins = vector.tensor_reduce(out=red8[:, d:d + 1],
                                                       in_=tmp[:, ds], axis=AX.X,
                                                       op=ALU.max,
                                                       apply_absolute_value=True)
                        vf(ins)
                        vf(vector.tensor_reduce(out=rt[:], in_=red8[:], axis=AX.X,
                                                op=ALU.max,
                                                apply_absolute_value=True))
                        vector.tensor_max(out=mxg[:], in0=mxg[:], in1=rt[:]
                                          ).then_inc(sv, 1)
                        vector.wait_ge(sv, V_MX(t))
                    vector.wait_ge(sp, P_FIN)
                    vector.tensor_reduce(out=sc[:], in_=psS[0][0:1, 0:128],
                                         axis=AX.X, op=ALU.max,
                                         apply_absolute_value=True
                                         ).then_inc(sv, 1)
                    vector.wait_ge(sv, V_SC)
                    vector.memset(gdv[:], 0.0).then_inc(bar, 1)
                @blk.gpsimd
                def _(gpsimd):
                    gpsimd = EW(gpsimd)
                    gpsimd.wait_ge(bar, NB0 + 19)
            bump()

        def emit_combine(site):
            NB0 = nbv[0]
            with ExitStack() as st:
                S = st.enter_context
                sc1 = S(nc.sbuf_tensor(f"c{site}sc", [1, 1], F32))
                fone = S(nc.sbuf_tensor(f"c{site}f1", [1, 1], I32))
                flp = S(nc.sbuf_tensor(f"c{site}fp", [1, 2], I32))
                pmaxf = S(nc.sbuf_tensor(f"c{site}pm", [1, 1], F32))
                pmaxg = S(nc.sbuf_tensor(f"c{site}pg", [1, 1], F32))
                cmax = S(nc.sbuf_tensor(f"c{site}cm", [1, 1], F32))
                mallsb = S(nc.sbuf_tensor(f"c{site}mb", [128, 1], F32))
                ld = GPOOL[0]
                wr = POOL[1]
                po = PODED[site]
                pv = GPOOL[1]
                cv = POOL[3]
                sq2 = GPOOL[2]

                @blk.vector
                def _(vector):
                    vector = EW(vector)
                    vector.wait_ge(bar, NB0)
                    vector.memset(fone[:], MAGICK)
                    vector.memset(flp[:], 0).then_inc(wr, 1)
                    vector.wait_ge(pv, 32)
                    vector.tensor_tensor(out=cmax[:], in0=pmaxf[:], in1=pmaxg[:],
                                         op=ALU.max).then_inc(cv, 1)
                    vector.wait_ge(cv, 1)
                    vector.tensor_tensor(out=cmax[:], in0=cmax[:], in1=sc1[:],
                                         op=ALU.max).then_inc(cv, 1)
                    vector.wait_ge(cv, 2)
                    vector.memset(gdv[:], 0.0).then_inc(bar, 1)

                @blk.gpsimd
                def _(gpsimd):
                    gpsimd = EW(gpsimd)
                    gpsimd.wait_ge(bar, NB0)
                    gpsimd.wait_ge(wr, 1)
                    gpsimd.dma_start(out=sc1[:], in_=mloc_t[site]).then_inc(ld, 16)
                    gpsimd.wait_ge(ld, 16)
                    my = ids["my"]
                    gpsimd.dma_start(out=shm_t[bass.ts(my, 1), site:site + 1],
                                     in_=sc1[:].bitcast(I32)).then_inc(ld, 16)
                    gpsimd.wait_ge(ld, 32)
                    gpsimd.dma_start(out=flg_t[bass.ts(my, 1), site:site + 1],
                                     in_=fone[:]).then_inc(ld, 16)
                    gpsimd.wait_ge(ld, 48)
                    with gpsimd.register() as rcnt, gpsimd.register() as rf, \
                         gpsimd.register() as rc:
                        gpsimd.reg_mov(rcnt, 0)
                        for row in range(2):

                            def cond(row=row):
                                gpsimd.reg_load(rf, flp[0:1, row:row + 1])
                                gpsimd.reg_alu(rc, rf, MAGICK, ALU.is_lt)
                                return rc

                            with gpsimd.While(cond):
                                gpsimd.reg_add(rcnt, rcnt, 16)
                                gpsimd.dma_start(
                                    out=flp[0:1, row:row + 1],
                                    in_=flg_t[row:row + 1, site:site + 1]
                                ).then_inc(po, 16)
                                gpsimd.wait_ge(po, rcnt)
                    gpsimd.dma_start(out=pmaxf[:].bitcast(I32),
                                     in_=shm_t[0:1, site:site + 1]
                                     ).then_inc(pv, 16)
                    gpsimd.wait_ge(pv, 16)
                    gpsimd.dma_start(out=pmaxg[:].bitcast(I32),
                                     in_=shm_t[1:2, site:site + 1]
                                     ).then_inc(pv, 16)
                    gpsimd.wait_ge(pv, 32)
                    gpsimd.wait_ge(cv, 2)
                    gpsimd.dma_start(out=cmd_t[site], in_=cmax[:]).then_inc(sq2, 16)
                    gpsimd.wait_ge(sq2, 16)
                    gpsimd.dma_start(out=mallsb[:],
                                     in_=cmd_t[site].broadcast_to([128, 1])
                                     ).then_inc(sq2, 16)
                    gpsimd.wait_ge(sq2, 32)
                    gpsimd.dma_start(out=mall_t[site], in_=mallsb[:]
                                     ).then_inc(sq2, 16)
                    gpsimd.wait_ge(sq2, 48)
                    gpsimd.memset(gdg[:], 0.0).then_inc(bar, 1)

                @blk.scalar
                def _(scalar):
                    scalar = EW(scalar)
                    scalar.wait_ge(bar, NB0)
                    scalar.activation(out=gda[:], in_=gds[:],
                                      func=AF.Identity).then_inc(bar, 1)

                @blk.sync
                def _(sync):
                    sync = EW(sync)
                    sync.wait_ge(bar, NB0)
                    sync.dma_start(out=gbar_t, in_=gds[:]).then_inc(bar, 16)
                @blk.gpsimd
                def _(gpsimd):
                    gpsimd = EW(gpsimd)
                    gpsimd.wait_ge(bar, NB0 + 19)
            bump()

        def emit_ffn_in(tag, b, src_ap, stats_t, mall_in, hdst_t, prequant,
                        zero_gmx, final, mloc_out):
            NB0 = nbv[0]
            B0 = b * TB
            hrow0 = b * H
            NG = HC * NT
            NB_ = NG // 4
            with ExitStack() as st:
                S = st.enter_context
                mrow_t2 = S(nc.sbuf_tensor(f"{tag}mrow", [1, TB], F32))
                rrow_t2 = S(nc.sbuf_tensor(f"{tag}rrow", [1, TB], F32))
                mrow = mrow_t2[0:1, :]
                rrow = rrow_t2[0:1, :]
                ubuf = [S(nc.sbuf_tensor(f"{tag}ub{i}", [128, TB], F32))
                        for i in range(2)]
                wbuf = [S(nc.sbuf_tensor(f"{tag}wb{i}", [128, H], F32))
                        for i in range(2)]
                xq = [S(nc.sbuf_tensor(f"{tag}xq{d}", [128, TB], BF16))
                      for d in range(DC)]
                wqs = [S(nc.sbuf_tensor(f"{tag}wqs{i}", [128, DC * 128], BF16))
                       for i in range(2)]
                wqo = [S(nc.sbuf_tensor(f"{tag}wqo{i}", [128, H], BF16))
                       for i in range(2)]
                wint = S(nc.sbuf_tensor(f"{tag}wint", [128, H // 4], I32))
                xint = S(nc.sbuf_tensor(f"{tag}xint", [128, TT], I32))
                qf = S(nc.sbuf_tensor(f"{tag}qf", [128, TT], F32))
                red8 = S(nc.sbuf_tensor(f"{tag}red8", [128, DC], F32))
                rt2 = S(nc.sbuf_tensor(f"{tag}rt2", [128, 1], F32))
                mallb = S(nc.sbuf_tensor(f"{tag}mallb", [128, 1], F32))
                b1c = S(nc.sbuf_tensor(f"{tag}b1c", [128, HC], F32))
                b1s = S(nc.sbuf_tensor(f"{tag}b1s", [128, HC], F32))
                tmpq = S(nc.sbuf_tensor(f"{tag}tmpq", [128, TB], F32))
                wmx = S(nc.sbuf_tensor(f"{tag}wmx", [128, 1], F32))
                mxm = S(nc.sbuf_tensor(f"{tag}mxm", [128, 1], F32))
                sxb = S(nc.sbuf_tensor(f"{tag}sxb", [128, 1], F32))
                rxb = S(nc.sbuf_tensor(f"{tag}rxb", [128, 1], F32))
                s1b = S(nc.sbuf_tensor(f"{tag}s1b", [128, 1], F32))
                swb = S(nc.sbuf_tensor(f"{tag}swb", [128, 1], F32))
                rwb = S(nc.sbuf_tensor(f"{tag}rwb", [128, 1], F32))
                sc1 = S(nc.sbuf_tensor(f"{tag}sc1", [128, 1], F32))
                dmy = S(nc.sbuf_tensor(f"{tag}dmy", [128, 1], F32))
                dmy2 = S(nc.sbuf_tensor(f"{tag}dmy2", [128, 1], F32))
                wsc = S(nc.sbuf_tensor(f"{tag}wsc", [1, 1], F32))
                sw_s = S(nc.sbuf_tensor(f"{tag}sws", [1, 1], F32))
                rw1 = S(nc.sbuf_tensor(f"{tag}rw1", [1, 1], F32))
                r1a = S(nc.sbuf_tensor(f"{tag}r1a", [1, 1], F32))
                srow = S(nc.sbuf_tensor(f"{tag}srow", [1, 2], F32))
                mgl = S(nc.sbuf_tensor(f"{tag}mgl", [1, 1], F32))
                pmm = [S(nc.psum_tensor(f"{tag}pmm{i}", [128, TT], F32))
                       for i in range(4)]
                psE = S(nc.psum_tensor(f"{tag}psE", [128, TT], F32))
                psF = S(nc.psum_tensor(f"{tag}psF", [128, TT], F32))
                paux = S(nc.psum_tensor(f"{tag}paux", [128, 128], F32))
                pbc = S(nc.psum_tensor(f"{tag}pbc", [128, 2], F32))
                lfix = POOL[0]
                lu = [POOL[1], POOL[2]]
                gw = [GPOOL[0], GPOOL[1]]
                gq = [GPOOL[2], GPOOL[3]]
                sq = [POOL[7], POOL[8]]
                sh = [POOL[9], POOL[10]]
                sws_sem = POOL[11]
                sv = POOL[12]
                sp = POOL[13]
                sa = POOL[14]
                so = POOL[15]
                vf_s = POOL[16]

                V_ONES = 1
                V_WCH = lambda d: 2 + d
                V_WMX = 10
                V_SROW = 11
                V_SCAL = 12
                V_WQ = lambda d: 13 + d
                V_XQ = lambda t: 21 + t
                V_G = lambda g: 29 + g
                V_GDONE = 29 + NG
                V_FIN = 30 + NG
                P_TR = 1
                P_BC = 2
                P_B = lambda t: 3 + t
                P_G = lambda g: 11 + g
                P_FINT = 11 + NG

                @blk.sync
                def _(sync):
                    sync = EW(sync)
                    sync.wait_ge(bar, NB0)
                    sync.dma_start(out=mrow_t2[:], in_=stats_t[0:1, B0:B0 + TB]
                                   ).then_inc(lfix, 16)
                    sync.dma_start(out=rrow_t2[:], in_=stats_t[1:2, B0:B0 + TB]
                                   ).then_inc(lfix, 16)
                    sync.dma_start(out=mallb[:], in_=mall_in).then_inc(lfix, 16)
                    sync.dma_start(out=b1c[:], in_=b1c_t).then_inc(lfix, 16)
                    if prequant:
                        sync.dma_start(out=swb[:], in_=swd1_t).then_inc(lfix, 16)
                    for t in range(NT):
                        if t >= 2:
                            sync.wait_ge(sv, V_XQ(t - 2))
                        sync.dma_start(
                            out=ubuf[t % 2][:].rearrange("p (c t) -> p c t", c=DC),
                            in_=src_ap(t).rearrange("(c p) t -> p c t", p=128),
                        ).then_inc(lu[t % 2], 16)
                    if final:
                        sync.wait_ge(sv, V_FIN)
                        sync.dma_start(out=mloc_out, in_=mgl[:]).then_inc(so, 16)
                        sync.wait_ge(so, 16)
                    else:
                        sync.wait_ge(sv, V_GDONE)
                    if not prequant:
                        sync.wait_ge(sq[0], 16 * (DC // 2))
                        sync.wait_ge(sq[1], 16 * (DC // 2))
                        sync.wait_ge(sws_sem, 16)
                    sync.wait_ge(sh[0], 16 * (NB_ // 2))
                    sync.wait_ge(sh[1], 16 * (NB_ // 2))
                    sync.dma_start(out=gbar_t, in_=gds[:]).then_inc(bar, 16)

                @blk.gpsimd
                def _(gpsimd):
                    gpsimd = EW(gpsimd)
                    gpsimd.wait_ge(bar, NB0)
                    if not prequant:
                        for d in range(DC):
                            if d >= 2:
                                gpsimd.wait_ge(sv, V_WCH(d - 2))
                            gpsimd.dma_start(out=wbuf[d % 2][:],
                                             in_=w1T_t[d * 128:(d + 1) * 128, :]
                                             ).then_inc(gw[d % 2], 16)
                        gpsimd.wait_ge(sv, V_WCH(DC - 1))
                        for d in range(DC):
                            if d >= 2:
                                gpsimd.wait_ge(sv, V_WQ(d - 2))
                            gpsimd.dma_start(out=wbuf[d % 2][:],
                                             in_=w1T_t[d * 128:(d + 1) * 128, :]
                                             ).then_inc(gw[d % 2], 16)
                        gpsimd.wait_ge(sq[0], 16 * (DC // 2))
                        gpsimd.wait_ge(sq[1], 16 * (DC // 2))
                    for h in range(HC):
                        if h >= 2:
                            gpsimd.wait_ge(sp, P_G((h - 1) * NT - 1))
                        gpsimd.dma_start(
                            out=wqs[h % 2][:].rearrange("p (c m) -> p c m", c=DC),
                            in_=wq1_t[:, h * 128:(h + 1) * 128].rearrange(
                                "(c p) m -> p c m", p=128),
                        ).then_inc(gq[h % 2], 16)
                    gpsimd.memset(gdg[:], 0.0).then_inc(bar, 1)

                @blk.vector
                def _(vector):
                    vector = EW(vector)
                    vector.wait_ge(bar, NB0)
                    vf = FE(vector, vf_s)
                    vf(vector.memset(dmy2[:], 0.0))
                    vector.memset(wmx[:], 0.0)
                    if zero_gmx:
                        vector.memset(gmxg[:], 0.0).then_inc(sv, 1)
                    else:
                        vector.memset(dmy[:], 0.0).then_inc(sv, 1)
                    vector.wait_ge(sv, V_ONES)
                    vector.wait_ge(lfix, 16 * (5 if prequant else 4))
                    vf(vector.tensor_reduce(out=mxm[:], in_=mallb[:], axis=AX.X,
                                            op=ALU.max, apply_absolute_value=True))
                    vf(vector.tensor_scalar(out=sxb[:], in0=mxm[:], scalar1=MINR,
                                            scalar2=1.0 / QP, op0=ALU.max,
                                            op1=ALU.mult))
                    recip_newton_f(vector, vf, rxb[:], sxb[:], sc1[:])
                    vf(vector.tensor_scalar(out=rrow, in0=rrow,
                                            scalar1=rxb[0:1, 0:1],
                                            scalar2=None, op0=ALU.mult))
                    if not prequant:
                        for d in range(DC):
                            vector.wait_ge(gw[d % 2], 16 * (d // 2 + 1))
                            vector.tensor_reduce(out=red8[:, d:d + 1],
                                                 in_=wbuf[d % 2][:], axis=AX.X,
                                                 op=ALU.max,
                                                 apply_absolute_value=True
                                                 ).then_inc(sv, 1)
                        vector.wait_ge(sv, V_WCH(DC - 1))
                        vector.tensor_reduce(out=wmx[:], in_=red8[:], axis=AX.X,
                                             op=ALU.max, apply_absolute_value=True
                                             ).then_inc(sv, 1)
                        vector.wait_ge(sv, V_WMX)
                        vector.wait_ge(sp, P_TR)
                        vf(vector.tensor_reduce(out=wsc[:], in_=paux[0:1, 0:128],
                                                axis=AX.X, op=ALU.max,
                                                apply_absolute_value=True))
                        vf(vector.tensor_scalar(out=sw_s[:], in0=wsc[:],
                                                scalar1=MINR, scalar2=1.0 / QP,
                                                op0=ALU.max, op1=ALU.mult))
                    else:
                        for _k in range(DC + 1):
                            vector.tensor_copy(out=dmy[:], in_=sxb[:]
                                               ).then_inc(sv, 1)
                            vector.wait_ge(sv, V_ONES + 1 + _k)
                        vector.wait_ge(sv, V_WMX)
                    if prequant:
                        recip_newton_f(vector, vf, rwb[:], swb[:], sc1[:])
                        vector.tensor_copy(out=dmy[:], in_=rwb[:]
                                           ).then_inc(sv, 1)
                        vector.wait_ge(sv, V_SROW)
                        vf(vector.tensor_tensor(out=s1b[:], in0=swb[:],
                                                in1=sxb[:], op=ALU.mult))
                        vector.tensor_scalar(out=b1s[:], in0=b1c[:],
                                             scalar1=s1b[:, 0:1], scalar2=None,
                                             op0=ALU.mult).then_inc(sv, 1)
                    else:
                        recip_newton_f(vector, vf, rw1[:], sw_s[:], r1a[:])
                        vector.tensor_copy(out=srow[0:1, 0:1], in_=sw_s[:])
                        vector.tensor_copy(out=srow[0:1, 1:2], in_=rw1[:]
                                           ).then_inc(sv, 1)
                        vector.wait_ge(sp, P_BC)
                        vector.tensor_copy(out=swb[:], in_=pbc[:, 0:1])
                        vf(vector.tensor_copy(out=rwb[:], in_=pbc[:, 1:2]))
                        vf(vector.tensor_tensor(out=s1b[:], in0=swb[:],
                                                in1=sxb[:], op=ALU.mult))
                        vector.tensor_scalar(out=b1s[:], in0=b1c[:],
                                             scalar1=s1b[:, 0:1], scalar2=None,
                                             op0=ALU.mult).then_inc(sv, 1)
                    for d in range(DC):
                        if prequant:
                            vector.tensor_copy(out=dmy[:], in_=rwb[:]
                                               ).then_inc(sv, 1)
                            vector.wait_ge(sv, V_WQ(d))
                            continue
                        vector.wait_ge(gw[d % 2], 16 * (4 + d // 2 + 1))
                        if d >= 2:
                            vector.wait_ge(sq[d % 2], 16 * (d // 2))
                        for k in range(4):
                            hs = slice(k * (H // 4), (k + 1) * (H // 4))
                            wf = tmpq[:, 0:H // 4]
                            vf(vector.tensor_scalar(out=wf, in0=wbuf[d % 2][:, hs],
                                                    scalar1=rwb[:, 0:1],
                                                    scalar2=MAGIC,
                                                    op0=ALU.mult, op1=ALU.add))
                            vf(vector.tensor_scalar(out=wint[:], in0=wf,
                                                    scalar1=MAGIC, scalar2=None,
                                                    op0=ALU.subtract))
                            ins = vector.tensor_copy(out=wqo[d % 2][:, hs],
                                                     in_=wint[:])
                            if k < 3:
                                vf(ins)
                        ins.then_inc(sv, 1)
                        vector.wait_ge(sv, V_WQ(d))
                    for t in range(NT):
                        ts_ = slice(t * TT, (t + 1) * TT)
                        vector.wait_ge(lu[t % 2], 16 * (t // 2 + 1))
                        vector.wait_ge(sp, P_B(t))
                        for d in range(DC):
                            ds = slice(d * TT, (d + 1) * TT)
                            ins = vector.tensor_sub(out=tmpq[:, ds],
                                                    in0=ubuf[t % 2][:, ds],
                                                    in1=psE[:])
                        vf(ins)
                        for d in range(DC):
                            ds = slice(d * TT, (d + 1) * TT)
                            vf(vector.tensor_tensor(out=qf[:], in0=tmpq[:, ds],
                                                    in1=psF[:], op=ALU.mult))
                            vf(vector.tensor_scalar(out=xint[:], in0=qf[:],
                                                    scalar1=MAGIC, scalar2=MAGIC,
                                                    op0=ALU.add,
                                                    op1=ALU.subtract))
                            ins = vector.tensor_copy(out=xq[d][:, ts_],
                                                     in_=xint[:])
                            if d < DC - 1:
                                vf(ins)
                        ins.then_inc(sv, 1)
                        vector.wait_ge(sv, V_XQ(t))
                    for g in range(NG):
                        vector.wait_ge(sa, g + 1)
                        stg = wbuf[(g // 4) % 2]
                        col = (g % 4) * TT
                        vector.tensor_reduce(out=red8[:, g % 8:g % 8 + 1],
                                             in_=stg[:, col:col + TT], axis=AX.X,
                                             op=ALU.max, apply_absolute_value=True
                                             ).then_inc(sv, 1)
                        if g % 8 == 7:
                            vector.wait_ge(sv, V_G(g))
                            vf(vector.tensor_reduce(out=rt2[:], in_=red8[:],
                                                    axis=AX.X, op=ALU.max,
                                                    apply_absolute_value=True))
                            if g == NG - 1:
                                vector.tensor_max(out=gmxg[:], in0=gmxg[:],
                                                  in1=rt2[:]).then_inc(sv, 1)
                            else:
                                vf(vector.tensor_max(out=gmxg[:], in0=gmxg[:],
                                                     in1=rt2[:]))
                    if final:
                        vector.wait_ge(sp, P_FINT)
                        vector.tensor_reduce(out=mgl[:], in_=paux[0:1, 0:128],
                                             axis=AX.X, op=ALU.max,
                                             apply_absolute_value=True
                                             ).then_inc(sv, 1)
                        vector.wait_ge(sv, V_FIN)
                    vector.memset(gdv[:], 0.0).then_inc(bar, 1)

                @blk.tensor
                def _(tensor):
                    tensor = EW(tensor)
                    tensor.wait_ge(bar, NB0)
                    tensor.wait_ge(lfix, 16 * (5 if prequant else 4))
                    if not prequant:
                        tensor.wait_ge(sv, V_WMX)
                        tensor.transpose(out=paux[0:1, 0:128], in_=wmx[:],
                                         identity=eyeg[:]).then_inc(sp, 1)
                        tensor.wait_ge(sv, V_SROW)
                        tensor.matmul(out=pbc[:], lhsT=onerg[:], rhs=srow[:],
                                      start=True, stop=True).then_inc(sp, 1)
                    else:
                        tensor.wait_ge(sv, V_ONES)
                        tensor.matmul(out=pbc[:, 0:1], lhsT=onerg[:],
                                      rhs=onerg[0:1, 0:1], start=True, stop=True
                                      ).then_inc(sp, 1)
                        tensor.wait_ge(sv, V_SROW)
                        tensor.matmul(out=pbc[:, 0:1], lhsT=onerg[:],
                                      rhs=onerg[0:1, 0:1], start=True, stop=True
                                      ).then_inc(sp, 1)
                    for t in range(NT):
                        ts_ = slice(t * TT, (t + 1) * TT)
                        if t >= 1:
                            tensor.wait_ge(sv, V_XQ(t - 1))
                        tensor.matmul(out=psE[:], lhsT=onerg[:],
                                      rhs=mrow[:, ts_], start=True, stop=True)
                        tensor.matmul(out=psF[:], lhsT=onerg[:],
                                      rhs=rrow[:, ts_], start=True, stop=True
                                      ).then_inc(sp, 1)
                    g = 0
                    for h in range(HC):
                        tensor.wait_ge(gq[h % 2], 16 * (h // 2 + 1))
                        for t in range(NT):
                            ts_ = slice(t * TT, (t + 1) * TT)
                            if h == 0:
                                tensor.wait_ge(sv, V_XQ(t))
                            if g >= 4:
                                tensor.wait_ge(sa, g - 3)
                            for d in range(DC):
                                ins = tensor.matmul(
                                    out=pmm[g % 4][:],
                                    lhsT=wqs[h % 2][:, d * 128:(d + 1) * 128],
                                    rhs=xq[d][:, ts_], start=(d == 0),
                                    stop=(d == DC - 1))
                            ins.then_inc(sp, 1)
                            g += 1
                    if final:
                        tensor.wait_ge(sv, V_GDONE)
                        tensor.transpose(out=paux[0:1, 0:128], in_=gmxg[:],
                                         identity=eyeg[:]).then_inc(sp, 1)

                @blk.scalar
                def _(scalar):
                    scalar = EW(scalar)
                    scalar.wait_ge(bar, NB0)
                    if not prequant:
                        scalar.wait_ge(sv, V_SCAL)
                        scalar.dma_start(out=swd1_t, in_=swb[:]
                                         ).then_inc(sws_sem, 16)
                        for d in range(DC):
                            scalar.wait_ge(sv, V_WQ(d))
                            scalar.dma_start(out=wq1_t[d * 128:(d + 1) * 128, :],
                                             in_=wqo[d % 2][:]
                                             ).then_inc(sq[d % 2], 16)
                    scalar.wait_ge(sv, V_SCAL)
                    scalar.wait_ge(sv, V_WQ(DC - 1))
                    g = 0
                    for h in range(HC):
                        for t in range(NT):
                            bb = g // 4
                            stg = wbuf[bb % 2]
                            col = (g % 4) * TT
                            if g % 4 == 0 and bb >= 2:
                                scalar.wait_ge(sh[bb % 2], 16 * (bb // 2))
                                scalar.wait_ge(sv, V_G(4 * (bb - 1) - 1))
                            scalar.wait_ge(sp, P_G(g))
                            scalar.activation(out=stg[:, col:col + TT],
                                              in_=pmm[g % 4][:], func=AF.Gelu,
                                              bias=b1s[:, h:h + 1],
                                              scale=s1b[:, 0:1]).then_inc(sa, 1)
                            if g % 4 == 3:
                                scalar.wait_ge(sa, g + 1)
                                th = (t - 3) // 4
                                scalar.dma_start(
                                    out=hdst_t[hrow0 + h * 128:hrow0 + (h + 1) * 128,
                                               th * 2048:(th + 1) * 2048],
                                    in_=stg[:, 0:2048]).then_inc(sh[bb % 2], 16)
                            g += 1
                    scalar.activation(out=gda[:], in_=gds[:],
                                      func=AF.Identity).then_inc(bar, 1)
                @blk.gpsimd
                def _(gpsimd):
                    gpsimd = EW(gpsimd)
                    gpsimd.wait_ge(bar, NB0 + 19)
            bump()

        def emit_ffn_out(tag, b, src_t, h_t, mall_in, dst_t, prequant,
                         hrow0=None, drow0=0):
            NB0 = nbv[0]
            B0 = b * TB
            if hrow0 is None:
                hrow0 = b * H
            NW = HC
            NQ = 4
            with ExitStack() as st:
                S = st.enter_context
                wbuf = [S(nc.sbuf_tensor(f"{tag}wb{i}", [128, D], F32))
                        for i in range(2)]
                wq2g = [S(nc.sbuf_tensor(f"{tag}wqg{i}", [128, 4 * D], BF16))
                        for i in range(NW // 4)]
                wq2r = [wq2g[h // 4][:, (h % 4) * D:(h % 4 + 1) * D]
                        for h in range(NW)]
                hbuf = [S(nc.sbuf_tensor(f"{tag}hb{i}", [128, DC * TT], F32))
                        for i in range(2)]
                hq = [S(nc.sbuf_tensor(f"{tag}hq{i}", [128, DC * TT], BF16))
                      for i in range(2)]
                ubuf = [S(nc.sbuf_tensor(f"{tag}ub{i}", [128, TB], F32))
                        for i in range(2)]
                xo = [S(nc.sbuf_tensor(f"{tag}xo{i}", [128, TB], F32))
                      for i in range(2)]
                wint = S(nc.sbuf_tensor(f"{tag}wint", [128, D], I32))
                wf = S(nc.sbuf_tensor(f"{tag}wf", [128, D], F32))
                xint = S(nc.sbuf_tensor(f"{tag}xint", [128, TT], I32))
                qf = S(nc.sbuf_tensor(f"{tag}qf", [128, TT], F32))
                mallb = S(nc.sbuf_tensor(f"{tag}mallb", [128, 1], F32))
                b2c = S(nc.sbuf_tensor(f"{tag}b2c", [128, DC], F32))
                b2s = S(nc.sbuf_tensor(f"{tag}b2s", [128, DC], F32))
                red8 = S(nc.sbuf_tensor(f"{tag}red8", [128, NW], F32))
                wmx = S(nc.sbuf_tensor(f"{tag}wmx", [128, 1], F32))
                mxm = S(nc.sbuf_tensor(f"{tag}mxm", [128, 1], F32))
                sxb = S(nc.sbuf_tensor(f"{tag}sxb", [128, 1], F32))
                rxb = S(nc.sbuf_tensor(f"{tag}rxb", [128, 1], F32))
                s5b = S(nc.sbuf_tensor(f"{tag}s5b", [128, 1], F32))
                swb = S(nc.sbuf_tensor(f"{tag}swb", [128, 1], F32))
                rwb = S(nc.sbuf_tensor(f"{tag}rwb", [128, 1], F32))
                sc1 = S(nc.sbuf_tensor(f"{tag}sc1", [128, 1], F32))
                dmy = S(nc.sbuf_tensor(f"{tag}dmy", [128, 1], F32))
                wsc = S(nc.sbuf_tensor(f"{tag}wsc", [1, 1], F32))
                sw_s = S(nc.sbuf_tensor(f"{tag}sws", [1, 1], F32))
                rw1 = S(nc.sbuf_tensor(f"{tag}rw1", [1, 1], F32))
                r1a = S(nc.sbuf_tensor(f"{tag}r1a", [1, 1], F32))
                srow = S(nc.sbuf_tensor(f"{tag}srow", [1, 2], F32))
                pmm = [S(nc.psum_tensor(f"{tag}pmm{i}", [128, TT], F32))
                       for i in range(DC)]
                lfix = POOL[0]
                lu = [POOL[1], POOL[2]]
                lh = [GPOOL[2], POOL[4]]
                gw = [GPOOL[0], GPOOL[1]]
                sx_ = [POOL[7], POOL[8]]
                swq = [POOL[9], POOL[10]]
                so_ = POOL[11]
                sv = POOL[12]
                sp = POOL[13]
                sa = POOL[14]
                vf_s = POOL[15]

                V_ONES = 1
                V_WCH = lambda h: 2 + h
                V_WMX = 2 + NW
                V_SROW = V_WMX + 1
                V_SCAL = V_SROW + 1
                V_WQ = lambda h: V_SCAL + 1 + h
                _pos = {}
                _ctr = V_WQ(NW - 1)
                for _t in range(NT):
                    for _q in range(NQ):
                        _ctr += 1
                        _pos[("q", _t, _q)] = _ctr
                    if _t >= 1:
                        _ctr += 1
                        _pos[("a", _t - 1)] = _ctr
                _ctr += 1
                _pos[("a", NT - 1)] = _ctr
                V_HQ = lambda t, q: _pos[("q", t, q)]
                V_XO = lambda t: _pos[("a", t)]
                V_FIN = V_XO(NT - 1)
                V_SWST = V_SROW
                P_TR = 1
                P_BC = 2
                P_MM = lambda t, q: 3 + t * NQ + q
                P_T = lambda t: P_MM(t, NQ - 1)

                @blk.sync
                def _(sync):
                    sync = EW(sync)
                    sync.wait_ge(bar, NB0)
                    sync.dma_start(out=mallb[:], in_=mall_in).then_inc(lfix, 16)
                    sync.dma_start(out=b2c[:], in_=b2c_t).then_inc(lfix, 16)
                    if prequant:
                        sync.dma_start(out=swb[:], in_=swd2_t).then_inc(lfix, 16)
                    for t in range(NT):
                        if t >= 2:
                            sync.wait_ge(sv, V_XO(t - 2))
                        sync.dma_start(
                            out=ubuf[t % 2][:].rearrange("p (c t) -> p c t", c=DC),
                            in_=src_t[:, B0 + t * TT:B0 + (t + 1) * TT].rearrange(
                                "(c p) t -> p c t", p=128),
                        ).then_inc(lu[t % 2], 16)
                        for j in (4 * t + 1, 4 * t + 3):
                            tq, qq = divmod(j, NQ)
                            if j >= 2:
                                sync.wait_ge(sv, V_HQ(*divmod(j - 2, NQ)))
                            sync.dma_start(
                                out=hbuf[1][:].rearrange("p (c t) -> p c t", c=DC),
                                in_=h_t[hrow0 + qq * (H // NQ):
                                        hrow0 + (qq + 1) * (H // NQ),
                                        tq * TT:(tq + 1) * TT
                                        ].rearrange("(c p) t -> p c t", p=128),
                            ).then_inc(lh[1], 16)
                    sync.wait_ge(sv, V_FIN)
                    if not prequant:
                        sync.wait_ge(sv, V_SWST)
                        sync.dma_start(out=swd2_t, in_=swb[:]).then_inc(so_, 16)
                        sync.wait_ge(so_, 16)
                        sync.wait_ge(swq[0], 16 * (NW // 2))
                        sync.wait_ge(swq[1], 16 * (NW // 2))
                    sync.wait_ge(sx_[0], 16 * (NT // 2))
                    sync.wait_ge(sx_[1], 16 * (NT // 2))
                    sync.dma_start(out=gbar_t, in_=gds[:]).then_inc(bar, 16)

                @blk.gpsimd
                def _(gpsimd):
                    gpsimd = EW(gpsimd)
                    gpsimd.wait_ge(bar, NB0)
                    if prequant:
                        for gidx in range(NW // 4):
                            if gidx >= 2:
                                gpsimd.wait_ge(gw[gidx % 2], 16 * (gidx // 2))
                            gpsimd.dma_start(
                                out=wq2g[gidx][:].rearrange("p (c m) -> p c m",
                                                            c=4),
                                in_=wq2_t[gidx * 512:(gidx + 1) * 512, :
                                          ].rearrange("(c p) m -> p c m", p=128),
                            ).then_inc(gw[gidx % 2], 16)
                    else:
                        for h in range(NW):
                            if h >= 2:
                                gpsimd.wait_ge(sv, V_WCH(h - 2))
                            gpsimd.dma_start(out=wbuf[h % 2][:],
                                             in_=w2T_t[h * 128:(h + 1) * 128, :]
                                             ).then_inc(gw[h % 2], 16)
                        gpsimd.wait_ge(sv, V_WCH(NW - 1))
                        for h in range(NW):
                            if h >= 2:
                                gpsimd.wait_ge(sv, V_WQ(h - 2))
                            gpsimd.dma_start(out=wbuf[h % 2][:],
                                             in_=w2T_t[h * 128:(h + 1) * 128, :]
                                             ).then_inc(gw[h % 2], 16)
                    for t in range(NT):
                        for q in range(NQ):
                            j = t * NQ + q
                            if j % 2 != 0:
                                continue
                            if j >= 2:
                                gpsimd.wait_ge(sv, V_HQ(*divmod(j - 2, NQ)))
                            gpsimd.dma_start(
                                out=hbuf[0][:].rearrange("p (c t) -> p c t", c=DC),
                                in_=h_t[hrow0 + q * (H // NQ):
                                        hrow0 + (q + 1) * (H // NQ),
                                        t * TT:(t + 1) * TT
                                        ].rearrange("(c p) t -> p c t", p=128),
                            ).then_inc(lh[0], 16)
                    gpsimd.memset(gdg[:], 0.0).then_inc(bar, 1)

                @blk.vector
                def _(vector):
                    vector = EW(vector)
                    vector.wait_ge(bar, NB0)
                    vf = FE(vector, vf_s)
                    vf(vector.memset(dmy[:], 0.0))
                    vector.memset(wmx[:], 0.0).then_inc(sv, 1)
                    vector.wait_ge(sv, V_ONES)
                    vector.wait_ge(lfix, 16 * (3 if prequant else 2))
                    vf(vector.tensor_reduce(out=mxm[:], in_=mallb[:], axis=AX.X,
                                            op=ALU.max, apply_absolute_value=True))
                    vf(vector.tensor_scalar(out=sxb[:], in0=mxm[:], scalar1=MINR,
                                            scalar2=1.0 / QP, op0=ALU.max,
                                            op1=ALU.mult))
                    recip_newton_f(vector, vf, rxb[:], sxb[:], sc1[:])
                    if not prequant:
                        for h in range(NW):
                            vector.wait_ge(gw[h % 2], 16 * (h // 2 + 1))
                            vector.tensor_reduce(out=red8[:, h:h + 1],
                                                 in_=wbuf[h % 2][:], axis=AX.X,
                                                 op=ALU.max,
                                                 apply_absolute_value=True
                                                 ).then_inc(sv, 1)
                        vector.wait_ge(sv, V_WCH(NW - 1))
                        vector.tensor_reduce(out=wmx[:], in_=red8[:], axis=AX.X,
                                             op=ALU.max, apply_absolute_value=True
                                             ).then_inc(sv, 1)
                        vector.wait_ge(sv, V_WMX)
                        vector.wait_ge(sp, P_TR)
                        vf(vector.tensor_reduce(out=wsc[:], in_=pmm[0][0:1, 0:128],
                                                axis=AX.X, op=ALU.max,
                                                apply_absolute_value=True))
                        vf(vector.tensor_scalar(out=sw_s[:], in0=wsc[:],
                                                scalar1=MINR, scalar2=1.0 / QP,
                                                op0=ALU.max, op1=ALU.mult))
                    else:
                        for _k in range(NW + 1):
                            vector.tensor_copy(out=dmy[:], in_=sxb[:]
                                               ).then_inc(sv, 1)
                            vector.wait_ge(sv, V_ONES + 1 + _k)
                        vector.wait_ge(sv, V_WMX)
                    if prequant:
                        recip_newton_f(vector, vf, rwb[:], swb[:], sc1[:])
                        vector.tensor_copy(out=dmy[:], in_=rwb[:]
                                           ).then_inc(sv, 1)
                        vector.wait_ge(sv, V_SROW)
                    else:
                        recip_newton_f(vector, vf, rw1[:], sw_s[:], r1a[:])
                        vector.tensor_copy(out=srow[0:1, 0:1], in_=sw_s[:])
                        vector.tensor_copy(out=srow[0:1, 1:2], in_=rw1[:]
                                           ).then_inc(sv, 1)
                        vector.wait_ge(sp, P_BC)
                        vector.tensor_copy(out=swb[:], in_=pmm[0][:, 0:1])
                        vf(vector.tensor_copy(out=rwb[:], in_=pmm[0][:, 1:2]))
                    vf(vector.tensor_tensor(out=s5b[:], in0=swb[:], in1=sxb[:],
                                            op=ALU.mult))
                    vf(vector.tensor_scalar(out=s5b[:], in0=s5b[:], scalar1=0.5,
                                            scalar2=None, op0=ALU.mult))
                    vector.tensor_scalar(out=b2s[:], in0=b2c[:],
                                         scalar1=s5b[:, 0:1], scalar2=None,
                                         op0=ALU.mult).then_inc(sv, 1)
                    for h in range(NW):
                        if prequant:
                            gidx = h // 4
                            vector.wait_ge(gw[gidx % 2], 16 * (gidx // 2 + 1))
                            vector.tensor_copy(out=dmy[:], in_=rwb[:]
                                               ).then_inc(sv, 1)
                            vector.wait_ge(sv, V_WQ(h))
                            continue
                        vector.wait_ge(gw[h % 2], 16 * (NW // 2 + h // 2 + 1))
                        vf(vector.tensor_scalar(out=wf[:], in0=wbuf[h % 2][:],
                                                scalar1=rwb[:, 0:1], scalar2=MAGIC,
                                                op0=ALU.mult, op1=ALU.add))
                        vf(vector.tensor_scalar(out=wint[:], in0=wf[:],
                                                scalar1=MAGIC, scalar2=None,
                                                op0=ALU.subtract))
                        vector.tensor_copy(out=wq2r[h], in_=wint[:]
                                           ).then_inc(sv, 1)
                        vector.wait_ge(sv, V_WQ(h))

                    def residual_add(t):
                        vector.wait_ge(lu[t % 2], 16 * (t // 2 + 1))
                        vector.wait_ge(sa, DC * (t + 1))
                        for d in range(DC):
                            ds = slice(d * TT, (d + 1) * TT)
                            ins = vector.tensor_add(out=xo[t % 2][:, ds],
                                                    in0=xo[t % 2][:, ds],
                                                    in1=ubuf[t % 2][:, ds])
                        ins.then_inc(sv, 1)
                        vector.wait_ge(sv, V_XO(t))

                    for t in range(NT):
                        for q in range(NQ):
                            j = t * NQ + q
                            i = j % 2
                            vector.wait_ge(lh[i], 16 * (j // 2 + 1))
                            if j >= 2:
                                vector.wait_ge(sp, P_MM(*divmod(j - 2, NQ)))
                            for c in range(DC):
                                cs = slice(c * TT, (c + 1) * TT)
                                vf(vector.tensor_scalar(out=qf[:],
                                                        in0=hbuf[i][:, cs],
                                                        scalar1=rxb[:, 0:1],
                                                        scalar2=MAGIC,
                                                        op0=ALU.mult,
                                                        op1=ALU.add))
                                vf(vector.tensor_scalar(out=xint[:], in0=qf[:],
                                                        scalar1=MAGIC,
                                                        scalar2=None,
                                                        op0=ALU.subtract))
                                ins = vector.tensor_copy(out=hq[i][:, cs],
                                                         in_=xint[:])
                                if c < DC - 1:
                                    vf(ins)
                            ins.then_inc(sv, 1)
                            vector.wait_ge(sv, V_HQ(t, q))
                        if t >= 1:
                            residual_add(t - 1)
                    residual_add(NT - 1)
                    vector.memset(gdv[:], 0.0).then_inc(bar, 1)

                @blk.tensor
                def _(tensor):
                    tensor = EW(tensor)
                    tensor.wait_ge(bar, NB0)
                    tensor.wait_ge(lfix, 16 * (3 if prequant else 2))
                    if not prequant:
                        tensor.wait_ge(sv, V_WMX)
                        tensor.transpose(out=pmm[0][0:1, 0:128], in_=wmx[:],
                                         identity=eyeg[:]).then_inc(sp, 1)
                    else:
                        tensor.wait_ge(sv, V_ONES)
                        tensor.matmul(out=pmm[0][0:1, 0:1], lhsT=onerg[:, 0:1],
                                      rhs=onerg[0:1, 0:1], start=True, stop=True
                                      ).then_inc(sp, 1)
                    tensor.wait_ge(sv, V_SROW)
                    if prequant:
                        tensor.matmul(out=pmm[0][0:1, 0:1], lhsT=onerg[:, 0:1],
                                      rhs=onerg[0:1, 0:1], start=True, stop=True
                                      ).then_inc(sp, 1)
                    else:
                        tensor.matmul(out=pmm[0][:, 0:2], lhsT=onerg[:],
                                      rhs=srow[:], start=True, stop=True
                                      ).then_inc(sp, 1)
                    for t in range(NT):
                        for q in range(NQ):
                            i = (t * NQ + q) % 2
                            if t == 0:
                                tensor.wait_ge(sv, V_WQ(q * DC + DC - 1))
                            tensor.wait_ge(sv, V_HQ(t, q))
                            if q == 0 and t >= 1:
                                tensor.wait_ge(sa, DC * t)
                            for hh in range(DC):
                                h = q * DC + hh
                                for d in range(DC):
                                    ins = tensor.matmul(
                                        out=pmm[d][:],
                                        lhsT=wq2r[h][:, d * 128:(d + 1) * 128],
                                        rhs=hq[i][:, hh * TT:(hh + 1) * TT],
                                        start=(h == 0), stop=(h == NW - 1))
                            ins.then_inc(sp, 1)

                @blk.scalar
                def _(scalar):
                    scalar = EW(scalar)
                    scalar.wait_ge(bar, NB0)
                    if not prequant:
                        for h in range(NW):
                            scalar.wait_ge(sv, V_WQ(h))
                            scalar.dma_start(out=wq2_t[h * 128:(h + 1) * 128, :],
                                             in_=wq2r[h]).then_inc(swq[h % 2], 16)
                    scalar.wait_ge(sv, V_SCAL)
                    for t in range(NT):
                        scalar.wait_ge(sp, P_T(t))
                        if t >= 2:
                            scalar.wait_ge(sx_[t % 2], 16 * (t // 2))
                        for d in range(DC):
                            ds = slice(d * TT, (d + 1) * TT)
                            scalar.activation(out=xo[t % 2][:, ds], in_=pmm[d][:],
                                              func=AF.Identity,
                                              bias=b2s[:, d:d + 1],
                                              scale=s5b[:, 0:1]).then_inc(sa, 1)
                        scalar.wait_ge(sv, V_XO(t))
                        scalar.dma_start(
                            out=dst_t[drow0:drow0 + D, t * TT:(t + 1) * TT
                                      ].rearrange("(c p) t -> p c t", p=128),
                            in_=xo[t % 2][:].rearrange("p (c t) -> p c t", c=DC),
                        ).then_inc(sx_[t % 2], 16)
                    scalar.activation(out=gda[:], in_=gds[:],
                                      func=AF.Identity).then_inc(bar, 1)
                @blk.gpsimd
                def _(gpsimd):
                    gpsimd = EW(gpsimd)
                    gpsimd.wait_ge(bar, NB0 + 19)
            bump()

        def emit_ssm_a(tag, q):
            NB0 = nbv[0]
            B0 = q * TB
            with ExitStack() as st:
                S = st.enter_context
                xbuf = [S(nc.sbuf_tensor(f"{tag}xb{i}", [128, TB], F32))
                        for i in range(2)]
                sq = [S(nc.sbuf_tensor(f"{tag}sq{i}", [128, TT], F32))
                      for i in range(2)]
                xh = [S(nc.sbuf_tensor(f"{tag}xh{d}", [128, TB], BF16))
                      for d in range(DC)]
                wab = [S(nc.sbuf_tensor(f"{tag}wa{d}", [128, D], BF16))
                       for d in range(DC)]
                wbb = [S(nc.sbuf_tensor(f"{tag}wbc{d}", [128, D], BF16))
                       for d in range(DC)]
                wld = [S(nc.sbuf_tensor(f"{tag}wld{i}", [128, D], F32))
                       for i in range(2)]
                alpha = S(nc.sbuf_tensor(f"{tag}al", [128, TB], F32))
                bu = S(nc.sbuf_tensor(f"{tag}bu", [128, TB], F32))
                hs0 = S(nc.sbuf_tensor(f"{tag}hs0", [128, TB], F32))
                Pb = xbuf[1]
                bac = S(nc.sbuf_tensor(f"{tag}bac", [128, DC], F32))
                bbc = S(nc.sbuf_tensor(f"{tag}bbc", [128, DC], F32))
                hend = S(nc.sbuf_tensor(f"{tag}hend", [128, DC], F32))
                t0 = S(nc.sbuf_tensor(f"{tag}t0", [1, TT], F32))
                t1 = S(nc.sbuf_tensor(f"{tag}t1", [1, TT], F32))
                vr = S(nc.sbuf_tensor(f"{tag}vr", [1, TT], F32))
                lnt = S(nc.sbuf_tensor(f"{tag}lnt", [1, TT], F32))
                mr = S(nc.sbuf_tensor(f"{tag}mr", [1, TT], F32))
                rr = S(nc.sbuf_tensor(f"{tag}rr", [1, TT], F32))
                tmpq = S(nc.sbuf_tensor(f"{tag}tmpq", [128, TT], F32))
                psS = [S(nc.psum_tensor(f"{tag}psS{i}", [128, TT], F32))
                       for i in range(2)]
                psM = [S(nc.psum_tensor(f"{tag}psM{i}", [128, TT], F32))
                       for i in range(2)]
                psA = [S(nc.psum_tensor(f"{tag}psA{i}", [128, TT], F32))
                       for i in range(2)]
                psB = [S(nc.psum_tensor(f"{tag}psB{i}", [128, TT], F32))
                       for i in range(2)]
                pA = [psA[0], psA[1], psS[0], psS[1]]
                pB = [psB[0], psB[1], psM[0], psM[1]]
                lfix = POOL[0]
                lx2 = [POOL[1], POOL[2]]
                gw = [GPOOL[0], GPOOL[1]]
                sh = [POOL[5], POOL[6]]
                srl = POOL[7]
                she = POOL[8]
                sv = POOL[9]
                sp = POOL[10]
                sa = POOL[11]
                vf_s = POOL[12]
                af_s = POOL[13]

                V_INIT = 1
                V_WB = lambda k: 2 + k
                V_ROW = lambda t: 18 + 3 * t
                V_VAR = lambda t: 19 + 3 * t
                V_XH = lambda t: 20 + 3 * t
                V_PHA = V_XH(NT - 1)
                V_Z = V_PHA + 1
                V_SC = lambda c: V_Z + 1 + 3 * c + 0
                V_PS = lambda c: V_Z + 2 + 3 * c
                V_HE = lambda c: V_Z + 3 + 3 * c
                V_FIN = V_HE(DC - 1)
                A_SQ = lambda t, d: 9 * t + d + 1
                A_RST = lambda t: 9 * t + 9
                A_EV = lambda c, tt: 9 * NT + 2 * (c * NT + tt) + 2
                P_S1 = lambda t: 10 * t + 1
                P_SQ = lambda t, d: 10 * t + 2 + d
                P_S2 = lambda t: 10 * t + 9
                P_BC = lambda t: 10 * t + 10
                P_AB = lambda c, tt: 10 * NT + c * NT + tt + 1

                @blk.sync
                def _(sync):
                    sync = EW(sync)
                    sync.wait_ge(bar, NB0)
                    sync.dma_start(out=bac[:], in_=bac_t).then_inc(lfix, 16)
                    sync.dma_start(out=bbc[:], in_=bbc_t).then_inc(lfix, 16)
                    for t in range(NT):
                        if t >= 2:
                            sync.wait_ge(sv, V_XH(t - 2))
                        sync.dma_start(
                            out=xbuf[t % 2][:].rearrange("p (c t) -> p c t", c=DC),
                            in_=x2_t[q * D:(q + 1) * D,
                                     t * TT:(t + 1) * TT].rearrange(
                                "(c p) t -> p c t", p=128),
                        ).then_inc(lx2[t % 2], 16)
                    sync.wait_ge(sv, V_FIN)
                    if q % 2 == 0:
                        sync.dma_start(out=hendd_t, in_=hend[:]).then_inc(she, 16)
                        sync.wait_ge(she, 16)
                    sync.wait_ge(srl, 16 * 2 * NT)
                    sync.wait_ge(sh[0], 16 * DC)
                    sync.wait_ge(sh[1], 16 * DC)
                    sync.dma_start(out=gbar_t, in_=gds[:]).then_inc(bar, 16)

                @blk.gpsimd
                def _(gpsimd):
                    gpsimd = EW(gpsimd)
                    gpsimd.wait_ge(bar, NB0)
                    for k in range(2 * DC):
                        if k >= 2:
                            gpsimd.wait_ge(sv, V_WB(k - 2))
                        src_tw = waT_t if k < DC else wbT_t
                        d = k % DC
                        gpsimd.dma_start(out=wld[k % 2][:],
                                         in_=src_tw[d * 128:(d + 1) * 128, :]
                                         ).then_inc(gw[k % 2], 16)
                    gpsimd.memset(gdg[:], 0.0).then_inc(bar, 1)

                @blk.vector
                def _(vector):
                    vector = EW(vector)
                    vector.wait_ge(bar, NB0)
                    vf = FE(vector, vf_s)
                    vector.memset(hend[:], 0.0)
                    vector.memset(t0[:], 0.0).then_inc(sv, 1)
                    vector.wait_ge(sv, V_INIT)
                    for k in range(2 * DC):
                        vector.wait_ge(gw[k % 2], 16 * (k // 2 + 1))
                        dst = wab[k] if k < DC else wbb[k - DC]
                        vector.tensor_copy(out=dst[:], in_=wld[k % 2][:]
                                           ).then_inc(sv, 1)
                    for t in range(NT):
                        ts_ = slice(t * TT, (t + 1) * TT)
                        vector.wait_ge(sp, P_S1(t))
                        if t >= 1:
                            vector.wait_ge(srl, 32 * t)
                        vf(vector.tensor_scalar(out=mr[:], in0=psS[t % 2][0:1, :],
                                                scalar1=INV_D, scalar2=None,
                                                op0=ALU.mult))
                        vector.wait_ge(sp, P_S2(t))
                        vector.tensor_scalar(out=t0[:], in0=psM[t % 2][0:1, :],
                                             scalar1=INV_D, scalar2=None,
                                             op0=ALU.mult).then_inc(sv, 1)
                        vf(vector.tensor_tensor(out=t1[:], in0=mr[:], in1=mr[:],
                                                op=ALU.mult))
                        vector.scalar_tensor_tensor(out=vr[:], in0=t0[:],
                                                    scalar=EPS, in1=t1[:],
                                                    op0=ALU.add, op1=ALU.subtract
                                                    ).then_inc(sv, 1)
                        vector.wait_ge(sv, V_VAR(t))
                        vector.wait_ge(sp, P_BC(t))
                        for d in range(DC):
                            ds = slice(d * TT, (d + 1) * TT)
                            vf(vector.tensor_sub(out=tmpq[:],
                                                 in0=xbuf[t % 2][:, ds],
                                                 in1=psA[t % 2][:]))
                            ins = vector.tensor_tensor(out=xh[d][:, ts_],
                                                       in0=tmpq[:],
                                                       in1=psB[t % 2][:],
                                                       op=ALU.mult)
                            if d < DC - 1:
                                vf(ins)
                        ins.then_inc(sv, 1)
                        vector.wait_ge(sv, V_XH(t))
                    vector.memset(xbuf[0][:], 0.0).then_inc(sv, 1)
                    vector.wait_ge(sv, V_Z)
                    for c in range(DC):
                        vector.wait_ge(sa, A_EV(c, NT - 1))
                        if c >= 1:
                            vector.wait_ge(sh[0], 16 * c)
                            vector.wait_ge(sh[1], 16 * c)
                        vector.tensor_tensor_scan(out=hs0[:], data0=alpha[:],
                                                  data1=bu[:], initial=0.0,
                                                  op0=ALU.mult, op1=ALU.add
                                                  ).then_inc(sv, 1)
                        vector.tensor_tensor_scan(out=Pb[:], data0=alpha[:],
                                                  data1=xbuf[0][:], initial=1.0,
                                                  op0=ALU.mult, op1=ALU.add
                                                  ).then_inc(sv, 1)
                        vector.wait_ge(sv, V_SC(c))
                        vector.tensor_copy(out=hend[:, c:c + 1],
                                           in_=hs0[:, TB - 1:TB]
                                           ).then_inc(sv, 1)
                        vector.wait_ge(sv, V_HE(c))
                    vector.memset(gdv[:], 0.0).then_inc(bar, 1)

                @blk.tensor
                def _(tensor):
                    tensor = EW(tensor)
                    tensor.wait_ge(bar, NB0)
                    tensor.wait_ge(sv, V_INIT)
                    for t in range(NT):
                        ts_ = slice(t * TT, (t + 1) * TT)
                        tensor.wait_ge(lx2[t % 2], 16 * (t // 2 + 1))
                        if t >= 2:
                            tensor.wait_ge(sv, V_XH(t - 2))
                        for d in range(DC):
                            ds = slice(d * TT, (d + 1) * TT)
                            ins = tensor.matmul(out=psS[t % 2][0:1, :],
                                                lhsT=onecg[:],
                                                rhs=xbuf[t % 2][:, ds],
                                                start=(d == 0),
                                                stop=(d == DC - 1))
                        ins.then_inc(sp, 1)
                        for d in range(DC):
                            tensor.wait_ge(sa, A_SQ(t, d))
                            tensor.matmul(out=psM[t % 2][0:1, :], lhsT=onecg[:],
                                          rhs=sq[d % 2][:], start=(d == 0),
                                          stop=(d == DC - 1)).then_inc(sp, 1)
                        tensor.wait_ge(sa, A_RST(t))
                        tensor.matmul(out=psA[t % 2][:], lhsT=onerg[:], rhs=mr[:],
                                      start=True, stop=True)
                        tensor.matmul(out=psB[t % 2][:], lhsT=onerg[:], rhs=rr[:],
                                      start=True, stop=True).then_inc(sp, 1)
                    tensor.wait_ge(sv, V_WB(2 * DC - 1))
                    tensor.wait_ge(sv, V_PHA)
                    for c in range(DC):
                        ns = slice(c * 128, (c + 1) * 128)
                        for tt in range(NT):
                            ts_ = slice(tt * TT, (tt + 1) * TT)
                            j = c * NT + tt
                            if j >= 4:
                                tensor.wait_ge(sa, A_EV(*divmod(j - 4, NT)))
                            for d in range(DC):
                                tensor.matmul(out=pA[j % 4][:], lhsT=wab[d][:, ns],
                                              rhs=xh[d][:, ts_], start=(d == 0),
                                              stop=(d == DC - 1))
                            for d in range(DC):
                                ins = tensor.matmul(out=pB[j % 4][:],
                                                    lhsT=wbb[d][:, ns],
                                                    rhs=xh[d][:, ts_],
                                                    start=(d == 0),
                                                    stop=(d == DC - 1))
                            ins.then_inc(sp, 1)

                @blk.scalar
                def _(scalar):
                    scalar = EW(scalar)
                    scalar.wait_ge(bar, NB0)
                    af = FE(scalar, af_s)
                    scalar.wait_ge(lfix, 32)
                    for t in range(NT):
                        ts_ = slice(t * TT, (t + 1) * TT)
                        scalar.wait_ge(lx2[t % 2], 16 * (t // 2 + 1))
                        if t >= 1:
                            scalar.wait_ge(srl, 32 * t)
                        for d in range(DC):
                            ds = slice(d * TT, (d + 1) * TT)
                            g = t * DC + d
                            if g >= 2:
                                scalar.wait_ge(sp, P_SQ((g - 2) // DC,
                                                        (g - 2) % DC))
                            scalar.activation(out=sq[d % 2][:],
                                              in_=xbuf[t % 2][:, ds],
                                              func=AF.Square).then_inc(sa, 1)
                        scalar.wait_ge(sv, V_VAR(t))
                        if t >= 1:
                            scalar.wait_ge(sa, A_RST(t - 1))
                        af(scalar.activation(out=lnt[:], in_=vr[:], func=AF.Ln))
                        scalar.activation(out=rr[:], in_=lnt[:], func=AF.Exp,
                                          scale=-0.5).then_inc(sa, 1)
                        scalar.wait_ge(sa, A_RST(t))
                        scalar.dma_start(out=stats2_t[0:1, B0 + t * TT:
                                                      B0 + (t + 1) * TT],
                                         in_=mr[:]).then_inc(srl, 16)
                        scalar.dma_start(out=stats2_t[1:2, B0 + t * TT:
                                                      B0 + (t + 1) * TT],
                                         in_=rr[:]).then_inc(srl, 16)
                    for c in range(DC):
                        if c >= 1:
                            scalar.wait_ge(sv, V_SC(c - 1))
                            scalar.dma_start(out=hs0_t[(c - 1) * 128:c * 128, :],
                                             in_=hs0[:]).then_inc(sh[0], 16)
                            scalar.wait_ge(sv, V_PS(c - 1))
                            scalar.dma_start(out=P_t[(c - 1) * 128:c * 128, :],
                                             in_=Pb[:]).then_inc(sh[1], 16)
                        for tt in range(NT):
                            ts_ = slice(tt * TT, (tt + 1) * TT)
                            j = c * NT + tt
                            scalar.wait_ge(sp, P_AB(c, tt))
                            if tt == 0 and c >= 1:
                                scalar.wait_ge(sv, V_PS(c - 1))
                            scalar.activation(out=alpha[:, ts_], in_=pA[j % 4][:],
                                              func=AF.Sigmoid,
                                              bias=bac[:, c:c + 1]
                                              ).then_inc(sa, 1)
                            scalar.activation(out=bu[:, ts_], in_=pB[j % 4][:],
                                              func=AF.Identity,
                                              bias=bbc[:, c:c + 1]
                                              ).then_inc(sa, 1)
                    scalar.wait_ge(sv, V_SC(DC - 1))
                    scalar.dma_start(out=hs0_t[(DC - 1) * 128:DC * 128, :],
                                     in_=hs0[:]).then_inc(sh[0], 16)
                    scalar.wait_ge(sv, V_PS(DC - 1))
                    scalar.dma_start(out=P_t[(DC - 1) * 128:DC * 128, :],
                                     in_=Pb[:]).then_inc(sh[1], 16)
                    scalar.activation(out=gda[:], in_=gds[:],
                                      func=AF.Identity).then_inc(bar, 1)
                @blk.gpsimd
                def _(gpsimd):
                    gpsimd = EW(gpsimd)
                    gpsimd.wait_ge(bar, NB0 + 19)
            bump()

        def emit_ssm_b(tag, q):
            NB0 = nbv[0]
            B0 = q * TB
            hin_src = zhin_t if q % 2 == 0 else hendd_t
            TT2 = 256
            NT2 = TB // TT2
            HT = TB // 2
            with ExitStack() as st:
                S = st.enter_context
                mrl = [S(nc.sbuf_tensor(f"{tag}mrl{i}", [1, TT2], F32))
                       for i in range(2)]
                rrl = [S(nc.sbuf_tensor(f"{tag}rrl{i}", [1, TT2], F32))
                       for i in range(2)]
                hsld = [S(nc.sbuf_tensor(f"{tag}hsl{i}", [128, HT], F32))
                        for i in range(2)]
                pld = [S(nc.sbuf_tensor(f"{tag}pld{i}", [128, HT], F32))
                       for i in range(2)]
                hsb = [S(nc.sbuf_tensor(f"{tag}hsb{c}", [128, TB], BF16))
                       for c in range(DC)]
                ctb = [S(nc.sbuf_tensor(f"{tag}ctb{c}", [128, D], BF16))
                       for c in range(DC)]
                cld = S(nc.sbuf_tensor(f"{tag}cld", [128, D], F32))
                x2b = [S(nc.sbuf_tensor(f"{tag}x2b{i}", [128, DC * TT2], F32))
                       for i in range(2)]
                ystg = [S(nc.sbuf_tensor(f"{tag}ys{i}", [128, DC * TT2], F32))
                        for i in range(2)]
                xhst = S(nc.sbuf_tensor(f"{tag}xhst", [128, DC * TT2], F32))
                hin = S(nc.sbuf_tensor(f"{tag}hin", [128, DC], F32))
                Dcb = S(nc.sbuf_tensor(f"{tag}Dc", [128, DC], F32))
                pbank = [S(nc.psum_tensor(f"{tag}pb{i}", [128, 2 * TT2], F32))
                         for i in range(4)]
                pmm = [pbank[i // 2][:, (i % 2) * TT2:(i % 2 + 1) * TT2]
                       for i in range(DC)]
                psmr = [S(nc.psum_tensor(f"{tag}pr{i}", [128, 2 * TT2], F32))
                        for i in range(2)]
                psM = [psmr[i][:, 0:TT2] for i in range(2)]
                psR = [psmr[i][:, TT2:2 * TT2] for i in range(2)]
                lfix = POOL[0]
                lx = [POOL[1], POOL[2]]
                lm = [POOL[3], POOL[4]]
                gc = [GPOOL[0], GPOOL[1]]
                glh2 = [GPOOL[2], GPOOL[3]]
                glp2 = [GPOOL[4], GPOOL[5]]
                sx_ = [POOL[11], POOL[12]]
                sv = POOL[13]
                sp = POOL[14]
                sa = POOL[15]
                vf_s = POOL[16]

                V_INIT = 1
                V_CB = lambda k: 2 + k
                V_HSB2 = lambda k: 10 + k
                V_HSB = lambda c: V_HSB2(2 * c + 1)
                V_X3 = lambda tt: 26 + tt
                V_FIN = V_X3(NT2 - 1)
                P_Y = lambda tt: tt + 1
                A_EV = lambda tt: 8 * (tt + 1)

                @blk.sync
                def _(sync):
                    sync = EW(sync)
                    sync.wait_ge(bar, NB0)
                    sync.dma_start(out=hin[:], in_=hin_src).then_inc(lfix, 16)
                    sync.dma_start(out=Dcb[:], in_=Dc_t).then_inc(lfix, 16)
                    for tt in range(NT2):
                        ts_ = slice(B0 + tt * TT2, B0 + (tt + 1) * TT2)
                        tsl = slice(tt * TT2, (tt + 1) * TT2)
                        if tt >= 2:
                            sync.wait_ge(sv, V_X3(tt - 2))
                            sync.wait_ge(sx_[tt % 2], 16 * (tt // 2))
                        sync.dma_start(
                            out=x2b[tt % 2][:].rearrange("p (c t) -> p c t", c=DC),
                            in_=x2_t[q * D:(q + 1) * D, tsl].rearrange(
                                "(c p) t -> p c t", p=128),
                        ).then_inc(lx[tt % 2], 16)
                        sync.dma_start(out=mrl[tt % 2][:], in_=stats2_t[0:1, ts_]
                                       ).then_inc(lm[tt % 2], 16)
                        sync.dma_start(out=rrl[tt % 2][:], in_=stats2_t[1:2, ts_]
                                       ).then_inc(lm[tt % 2], 16)
                    sync.wait_ge(sv, V_FIN)
                    sync.wait_ge(sx_[0], 16 * (NT2 // 2))
                    sync.wait_ge(sx_[1], 16 * (NT2 // 2))
                    sync.dma_start(out=gbar_t, in_=gds[:]).then_inc(bar, 16)

                @blk.gpsimd
                def _(gpsimd):
                    gpsimd = EW(gpsimd)
                    gpsimd.wait_ge(bar, NB0)
                    for k in range(DC):
                        if k >= 1:
                            gpsimd.wait_ge(sv, V_CB(k - 1))
                        gpsimd.dma_start(out=cld[:],
                                         in_=cT_t[k * 128:(k + 1) * 128, :]
                                         ).then_inc(gc[k % 2], 16)
                    for k in range(2 * DC):
                        c, half = divmod(k, 2)
                        if k >= 2:
                            gpsimd.wait_ge(sv, V_HSB2(k - 2))
                        hsl = slice(half * HT, (half + 1) * HT)
                        gpsimd.dma_start(out=hsld[k % 2][:],
                                         in_=hs0_t[c * 128:(c + 1) * 128, hsl]
                                         ).then_inc(glh2[k % 2], 16)
                        gpsimd.dma_start(out=pld[k % 2][:],
                                         in_=P_t[c * 128:(c + 1) * 128, hsl]
                                         ).then_inc(glp2[k % 2], 16)
                    gpsimd.memset(gdg[:], 0.0).then_inc(bar, 1)

                @blk.vector
                def _(vector):
                    vector = EW(vector)
                    vector.wait_ge(bar, NB0)
                    vf = FE(vector, vf_s)
                    vector.memset(xhst[:], 0.0).then_inc(sv, 1)
                    vector.wait_ge(sv, V_INIT)
                    vector.wait_ge(lfix, 32)
                    for k in range(DC):
                        vector.wait_ge(gc[k % 2], 16 * (k // 2 + 1))
                        vector.tensor_copy(out=ctb[k][:], in_=cld[:]
                                           ).then_inc(sv, 1)
                    for k in range(2 * DC):
                        c, half = divmod(k, 2)
                        hsl = slice(half * HT, (half + 1) * HT)
                        vector.wait_ge(glh2[k % 2], 16 * (k // 2 + 1))
                        vector.wait_ge(glp2[k % 2], 16 * (k // 2 + 1))
                        vector.scalar_tensor_tensor(out=hsb[c][:, hsl],
                                                    in0=pld[k % 2][:],
                                                    scalar=hin[:, c:c + 1],
                                                    in1=hsld[k % 2][:],
                                                    op0=ALU.mult, op1=ALU.add
                                                    ).then_inc(sv, 1)
                        vector.wait_ge(sv, V_HSB2(k))
                    for tt in range(NT2):
                        vector.wait_ge(lx[tt % 2], 16 * (tt // 2 + 1))
                        vector.wait_ge(sp, P_Y(tt))
                        vector.wait_ge(sa, A_EV(tt))
                        for d in range(DC):
                            ds = slice(d * TT2, (d + 1) * TT2)
                            ins = vector.tensor_sub(out=xhst[:, ds],
                                                    in0=x2b[tt % 2][:, ds],
                                                    in1=psM[tt % 2])
                        vf(ins)
                        for d in range(DC):
                            ds = slice(d * TT2, (d + 1) * TT2)
                            ins = vector.tensor_tensor(out=xhst[:, ds],
                                                       in0=xhst[:, ds],
                                                       in1=psR[tt % 2],
                                                       op=ALU.mult)
                        vf(ins)
                        for d in range(DC):
                            ds = slice(d * TT2, (d + 1) * TT2)
                            ins = vector.scalar_tensor_tensor(
                                out=ystg[tt % 2][:, ds], in0=xhst[:, ds],
                                scalar=Dcb[:, d:d + 1], in1=ystg[tt % 2][:, ds],
                                op0=ALU.mult, op1=ALU.add)
                        vf(ins)
                        for d in range(DC):
                            ds = slice(d * TT2, (d + 1) * TT2)
                            ins = vector.tensor_add(out=x2b[tt % 2][:, ds],
                                                    in0=ystg[tt % 2][:, ds],
                                                    in1=x2b[tt % 2][:, ds])
                        ins.then_inc(sv, 1)
                        vector.wait_ge(sv, V_X3(tt))
                    vector.memset(gdv[:], 0.0).then_inc(bar, 1)

                @blk.tensor
                def _(tensor):
                    tensor = EW(tensor)
                    tensor.wait_ge(bar, NB0)
                    tensor.wait_ge(lfix, 32)
                    tensor.wait_ge(sv, V_HSB(DC - 1))
                    for tt in range(NT2):
                        ts_ = slice(tt * TT2, (tt + 1) * TT2)
                        if tt >= 1:
                            tensor.wait_ge(sa, A_EV(tt - 1))
                        if tt >= 2:
                            tensor.wait_ge(sv, V_X3(tt - 2))
                        tensor.wait_ge(lm[tt % 2], 16 * (2 * (tt // 2) + 2))
                        for d in range(DC):
                            for c in range(DC):
                                tensor.matmul(out=pmm[d],
                                              lhsT=ctb[c][:, d * 128:(d + 1) * 128],
                                              rhs=hsb[c][:, ts_], start=(c == 0),
                                              stop=(c == DC - 1))
                        tensor.matmul(out=psM[tt % 2], lhsT=onerg[:],
                                      rhs=mrl[tt % 2][:], start=True, stop=True)
                        tensor.matmul(out=psR[tt % 2], lhsT=onerg[:],
                                      rhs=rrl[tt % 2][:], start=True, stop=True
                                      ).then_inc(sp, 1)

                @blk.scalar
                def _(scalar):
                    scalar = EW(scalar)
                    scalar.wait_ge(bar, NB0)
                    for tt in range(NT2):
                        scalar.wait_ge(sp, P_Y(tt))
                        if tt >= 2:
                            scalar.wait_ge(sv, V_X3(tt - 2))
                        for d in range(DC):
                            ds = slice(d * TT2, (d + 1) * TT2)
                            scalar.activation(out=ystg[tt % 2][:, ds], in_=pmm[d],
                                              func=AF.Identity).then_inc(sa, 1)
                        scalar.wait_ge(sv, V_X3(tt))
                        scalar.dma_start(
                            out=x3_t[q * D:(q + 1) * D,
                                     tt * TT2:(tt + 1) * TT2
                                     ].rearrange("(c p) t -> p c t", p=128),
                            in_=x2b[tt % 2][:].rearrange("p (c t) -> p c t", c=DC),
                        ).then_inc(sx_[tt % 2], 16)
                    scalar.activation(out=gda[:], in_=gds[:],
                                      func=AF.Identity).then_inc(bar, 1)
                @blk.gpsimd
                def _(gpsimd):
                    gpsimd = EW(gpsimd)
                    gpsimd.wait_ge(bar, NB0 + 19)
            bump()

        def emit_own_copy(tag, pairs):
            NB0 = nbv[0]
            with ExitStack() as st:
                S = st.enter_context
                cp = GPOOL[0]

                @blk.gpsimd
                def _(gpsimd):
                    gpsimd = EW(gpsimd)
                    gpsimd.wait_ge(bar, NB0)
                    n = 0
                    for dst_t, src_t, nrows, okey in pairs:
                        n += 16
                        gpsimd.dma_start(
                            out=dst_t[0:nrows, :],
                            in_=src_t[bass.ds(ids[okey], nrows), :],
                        ).then_inc(cp, 16)
                        gpsimd.wait_ge(cp, n)
                    gpsimd.memset(gdg[:], 0.0).then_inc(bar, 1)

                @blk.vector
                def _(vector):
                    vector = EW(vector)
                    vector.wait_ge(bar, NB0)
                    vector.memset(gdv[:], 0.0).then_inc(bar, 1)

                @blk.scalar
                def _(scalar):
                    scalar = EW(scalar)
                    scalar.wait_ge(bar, NB0)
                    scalar.activation(out=gda[:], in_=gds[:],
                                      func=AF.Identity).then_inc(bar, 1)

                @blk.sync
                def _(sync):
                    sync = EW(sync)
                    sync.wait_ge(bar, NB0)
                    sync.dma_start(out=gbar_t, in_=gds[:]).then_inc(bar, 16)
                @blk.gpsimd
                def _(gpsimd):
                    gpsimd = EW(gpsimd)
                    gpsimd.wait_ge(bar, NB0 + 19)
            bump()

        emit_stats("p1", lambda t: u_t[:, t * TT:(t + 1) * TT],
                   stats1_t, mloc_t[0])
        emit_combine(0)
        for b in range(NBK):
            emit_ffn_in(f"p2b{b}", b,
                        (lambda b2: lambda t: u_t[:, b2 * TB + t * TT:
                                                  b2 * TB + (t + 1) * TT])(b),
                        stats1_t, mall_t[0], h1_t,
                        prequant=(b > 0), zero_gmx=(b == 0), final=(b == NBK - 1),
                        mloc_out=mloc_t[1])
        emit_combine(1)
        for b in range(NBK):
            emit_ffn_out(f"p3b{b}", b, u_t, h1_t, mall_t[1], x2_t,
                         prequant=(b > 0), drow0=b * D)
        for q in range(NBK):
            emit_ssm_a(f"sa{q}", q)
            emit_ssm_b(f"sb{q}", q)
        emit_own_copy("oc1", [(x3own_t, x3_t, D, "offd")])
        emit_stats("p4",
                   lambda t: x3_t[(t // NT) * D:(t // NT + 1) * D,
                                  (t % NT) * TT:(t % NT + 1) * TT],
                   stats3_t, mloc_t[2])
        emit_combine(2)
        for b in range(NBK):
            emit_ffn_in(f"p5b{b}", b,
                        (lambda b2: lambda t: x3_t[b2 * D:(b2 + 1) * D,
                                                   t * TT:(t + 1) * TT])(b),
                        stats3_t, mall_t[2], h1_t,
                        prequant=True, zero_gmx=(b == 0), final=(b == NBK - 1),
                        mloc_out=mloc_t[3])
        emit_combine(3)
        emit_own_copy("oc2", [(h1own_t, h1_t, H, "offh")])
        emit_ffn_out("p6", 0, x3own_t, h1own_t, mall_t[3], xo_t,
                     prequant=True, hrow0=0, drow0=0)

    return nc



def _asc(a):
    return np.ascontiguousarray(a)


_EYE = np.eye(128, dtype=np.float32)


def _bcol(v):
    return np.ascontiguousarray(np.asarray(v).reshape(-1, 128).T)


def kernel(**inputs):
    u = np.asarray(inputs["L0_attn_u"])
    w1T = _asc(np.asarray(inputs["L0_ffn_w1"]).T)
    w2T = _asc(np.asarray(inputs["L0_ffn_w2"]).T)
    waT = _asc(np.asarray(inputs["L0_attn_W_alpha_w"]).T)
    wbT = _asc(np.asarray(inputs["L0_attn_W_B_w"]).T)
    cT = _asc(np.asarray(inputs["L0_attn_C_w"]).T)
    b1c = _bcol(inputs["L0_ffn_b1"])
    b2c = _bcol(inputs["L0_ffn_b2"])
    bac = _bcol(inputs["L0_attn_W_alpha_b"])
    bbc = _bcol(inputs["L0_attn_W_B_b"])
    Dcc = _bcol(inputs["L0_attn_D"])
    zhin = np.zeros((128, DC), np.float32)
    cores = list(range(NCORE))

    nc = build_mega()
    in_maps = []
    for c in cores:
        pb = (0, 1) if c % 2 == 0 else (2, 3)
        uc = _asc(np.concatenate([u[pb[0]].T, u[pb[1]].T], axis=1))
        own_off = (c // 2) * TB
        in_maps.append({
            "u": uc, "w1T": w1T, "w2T": w2T, "waT": waT, "wbT": wbT,
            "cT": cT, "b1c": b1c, "b2c": b2c, "bac": bac, "bbc": bbc,
            "Dc": Dcc, "eye": _EYE,
            "slot": np.array([[c % 2, 1 - c % 2]], np.int32),
            "zhin": zhin,
            "ooff": np.array([[(c // 2) * D, (c // 2) * H]], np.int32),
        })
    r = run_bass_kernel_spmd(nc, in_maps, core_ids=cores)

    out = np.empty((4, 8192, D), np.float32)
    for c in cores:
        pb = (0, 1) if c % 2 == 0 else (2, 3)
        own_off = (c // 2) * TB
        batch = pb[own_off // 8192]
        toff = own_off % 8192
        out[batch, toff:toff + TB, :] = np.asarray(r.results[c]["xo"]).T
    return out


# version 12
# speedup vs baseline: 1.0412x; 1.0412x over previous
import numpy as np
import ml_dtypes
from contextlib import ExitStack

import concourse.bass as bass
import concourse.mybir as mybir
from concourse.bass_utils import run_bass_kernel_spmd

F32 = mybir.dt.float32
BF16 = mybir.dt.bfloat16
I32 = mybir.dt.int32
AF = mybir.ActivationFunctionType
ALU = mybir.AluOpType
AX = mybir.AxisListType

NCORE = 8
D = 1024
H = 4096
DC = D // 128
HC = H // 128
TT = 512
TB = 4096
NT = TB // TT
NBK = 4
TP = TB * NBK
EPS = 1e-5
QP = 127.0
MAGIC = 12582912.0
MAGICK = 0x51C0DE
MINR = 1e-6
INV_D = 1.0 / D


class RawScope:

    def __enter__(self):
        return self

    def __exit__(self, *a):
        return False

    def enter_context(self, cm):
        return cm.__enter__()


class EngWrap:

    def __init__(self, eng, val, hi, exempt):
        object.__setattr__(self, "_e", eng)
        object.__setattr__(self, "_v", val)
        object.__setattr__(self, "_h", hi)
        object.__setattr__(self, "_x", exempt)

    def __getattr__(self, name):
        return getattr(self._e, name)

    def wait_ge(self, sem, v):
        num = sem.num
        if num in self._x or not isinstance(v, int):
            return self._e.wait_ge(sem, v)
        a = self._v.get(num, 0) + v
        if a > self._h.get(num, 0):
            self._h[num] = a
        return self._e.wait_ge(sem, a)


class FE:

    def __init__(self, eng, sem):
        self.eng, self.sem, self.n = eng, sem, 0

    def __call__(self, ins):
        self.n += 1
        ins.then_inc(self.sem, 1)
        self.eng.wait_ge(self.sem, self.n)


def recip_newton_f(vec, fe, out, in_, scr):
    fe(vec.reciprocal(out=out, in_=in_))
    fe(vec.tensor_tensor(out=scr, in0=in_, in1=out, op=ALU.mult))
    fe(vec.tensor_scalar(out=scr, in0=scr, scalar1=-1.0, scalar2=2.0,
                         op0=ALU.mult, op1=ALU.add))
    fe(vec.tensor_tensor(out=out, in0=out, in1=scr, op=ALU.mult))


def build_mega():
    nc = bass.Bass("TRN2", target_bir_lowering=False)
    u_t = nc.dram_tensor("u", [D, TP], F32, kind="ExternalInput").ap()
    w1T_t = nc.dram_tensor("w1T", [D, H], F32, kind="ExternalInput").ap()
    w2T_t = nc.dram_tensor("w2T", [H, D], F32, kind="ExternalInput").ap()
    waT_t = nc.dram_tensor("waT", [D, D], F32, kind="ExternalInput").ap()
    wbT_t = nc.dram_tensor("wbT", [D, D], F32, kind="ExternalInput").ap()
    cT_t = nc.dram_tensor("cT", [D, D], F32, kind="ExternalInput").ap()
    b1c_t = nc.dram_tensor("b1c", [128, HC], F32, kind="ExternalInput").ap()
    b2c_t = nc.dram_tensor("b2c", [128, DC], F32, kind="ExternalInput").ap()
    bac_t = nc.dram_tensor("bac", [128, DC], F32, kind="ExternalInput").ap()
    bbc_t = nc.dram_tensor("bbc", [128, DC], F32, kind="ExternalInput").ap()
    Dc_t = nc.dram_tensor("Dc", [128, DC], F32, kind="ExternalInput").ap()
    eye_t = nc.dram_tensor("eye", [128, 128], F32, kind="ExternalInput").ap()
    slot_t = nc.dram_tensor("slot", [1, 2], I32, kind="ExternalInput").ap()
    zhin_t = nc.dram_tensor("zhin", [128, DC], F32, kind="ExternalInput").ap()
    ooff_t = nc.dram_tensor("ooff", [1, 2], I32, kind="ExternalInput").ap()
    xo_t = nc.dram_tensor("xo", [D, TB], F32, kind="ExternalOutput").ap()
    stats1_t = nc.dram_tensor("stats1", [2, TP], F32, kind="Internal").ap()
    stats2_t = nc.dram_tensor("stats2", [2, TP], F32, kind="Internal").ap()
    stats3_t = nc.dram_tensor("stats3", [2, TP], F32, kind="Internal").ap()
    h1_t = nc.dram_tensor("h1d", [NBK * H, TB], F32, kind="Internal").ap()
    x2_t = nc.dram_tensor("x2d", [NBK * D, TB], F32, kind="Internal").ap()
    x3_t = nc.dram_tensor("x3d", [NBK * D, TB], F32, kind="Internal").ap()
    hs0_t = nc.dram_tensor("hs0d", [D, TB], F32, kind="Internal").ap()
    P_t = nc.dram_tensor("Pd", [D, TB], F32, kind="Internal").ap()
    hendd_t = nc.dram_tensor("hendd", [128, DC], F32, kind="Internal").ap()
    x3own_t = nc.dram_tensor("x3own", [D, TB], F32, kind="Internal").ap()
    h1own_t = nc.dram_tensor("h1own", [H, TB], F32, kind="Internal").ap()
    wq1_t = nc.dram_tensor("wq1", [D, H], BF16, kind="Internal").ap()
    wq2_t = nc.dram_tensor("wq2", [H, D], BF16, kind="Internal").ap()
    swd1_t = nc.dram_tensor("swd1", [128, 1], F32, kind="Internal").ap()
    swd2_t = nc.dram_tensor("swd2", [128, 1], F32, kind="Internal").ap()
    mloc_t = [nc.dram_tensor(f"mloc{k}", [1, 1], F32, kind="Internal").ap()
              for k in range(4)]
    mall_t = [nc.dram_tensor(f"mall{k}", [128, 1], F32, kind="Internal").ap()
              for k in range(4)]
    cmd_t = [nc.dram_tensor(f"cmd{k}", [1, 1], F32, kind="Internal").ap()
             for k in range(4)]
    gbar_t = nc.dram_tensor("gbar", [1, 1], F32, kind="Internal").ap()
    shm_t = nc.dram_tensor("shm", [2, 8], I32, kind="Internal",
                           addr_space="Shared").ap()
    flg_t = nc.dram_tensor("flg", [2, 8], I32, kind="Internal",
                           addr_space="Shared").ap()

    gstk = ExitStack()
    with gstk:
        G = gstk.enter_context
        gdv = G(nc.sbuf_tensor("gdv", [1, 1], F32))
        gda = G(nc.sbuf_tensor("gda", [1, 1], F32))
        gdg = G(nc.sbuf_tensor("gdg", [1, 1], F32))
        gds = G(nc.sbuf_tensor("gds", [1, 1], F32))
        mxg = G(nc.sbuf_tensor("mxg", [128, 1], F32))
        gmxg = G(nc.sbuf_tensor("gmxg", [128, 1], F32))
        eyeg = G(nc.sbuf_tensor("eyeg", [128, 128], F32))
        onecg = G(nc.sbuf_tensor("onecg", [128, 1], F32))
        onerg = G(nc.sbuf_tensor("onerg", [1, 128], F32))
        ssg = G(nc.sbuf_tensor("ssg", [1, 2], I32))
        oqg = G(nc.sbuf_tensor("oqg", [1, 2], I32))
        bar = G(nc.semaphore("bar"))
        POOL = [nc.semaphore(f"gp{i}").__enter__() for i in range(17)]
        GPOOL = [nc.semaphore(f"gg{i}").__enter__() for i in range(6)]
        PODED = [nc.semaphore(f"pod{k}").__enter__() for k in range(4)]
        blk = G(nc.Block())

        nbv = [0]
        ids = {}
        SVAL = {}
        SHI = {}
        EXEMPT = {bar.num} | {p.num for p in PODED}

        def EW(eng):
            return EngWrap(eng, SVAL, SHI, EXEMPT)

        def bump(n=19):
            nbv[0] += n
            SVAL.update(SHI)

        with ExitStack() as st:
            S = st.enter_context
            li = POOL[0]
            li2 = GPOOL[0]

            @blk.sync
            def _(sync):
                sync = EW(sync)
                sync.dma_start(out=eyeg[:], in_=eye_t).then_inc(li, 16)

            @blk.gpsimd
            def _(gpsimd):
                gpsimd = EW(gpsimd)
                gpsimd.dma_start(out=ssg[:], in_=slot_t).then_inc(li2, 16)
                gpsimd.dma_start(out=oqg[:], in_=ooff_t).then_inc(li2, 16)
                gpsimd.wait_ge(li2, 32)
                ids["my"] = gpsimd.value_load(ssg[0:1, 0:1])
                ids["offd"] = gpsimd.value_load(oqg[0:1, 0:1])
                ids["offh"] = gpsimd.value_load(oqg[0:1, 1:2])

            @blk.vector
            def _(vector):
                vector = EW(vector)
                vector.wait_ge(li, 16)
                vector.wait_ge(li2, 32)
                ids["voffd"] = vector.value_load(oqg[0:1, 0:1])
                ids["voffh"] = vector.value_load(oqg[0:1, 1:2])
                vector.memset(gds[:], 0.0)
                vector.memset(onecg[:], 1.0)
                vector.memset(onerg[:], 1.0).then_inc(bar, 1)

            @blk.gpsimd
            def _(gpsimd):
                gpsimd = EW(gpsimd)
                gpsimd.wait_ge(bar, 1)
        nbv[0] = 1
        SVAL.update(SHI)

        def emit_stats(tag, src_ap, stats_t, mloc_out):
            NB0 = nbv[0]
            NTT = TP // TT
            with ExitStack() as st:
                S = st.enter_context
                ub = [S(nc.sbuf_tensor(f"{tag}ub{i}", [128, DC * TT], F32))
                      for i in range(2)]
                sqt = [S(nc.sbuf_tensor(f"{tag}sq{i}", [128, TT], F32))
                       for i in range(2)]
                mr = S(nc.sbuf_tensor(f"{tag}mr", [1, TT], F32))
                rr = S(nc.sbuf_tensor(f"{tag}rr", [1, TT], F32))
                t0 = S(nc.sbuf_tensor(f"{tag}t0", [1, TT], F32))
                t1 = S(nc.sbuf_tensor(f"{tag}t1", [1, TT], F32))
                vr = S(nc.sbuf_tensor(f"{tag}vr", [1, TT], F32))
                lnt = S(nc.sbuf_tensor(f"{tag}lnt", [1, TT], F32))
                tmp = S(nc.sbuf_tensor(f"{tag}tmp", [128, DC * TT], F32))
                red8 = S(nc.sbuf_tensor(f"{tag}red8", [128, DC], F32))
                rt = S(nc.sbuf_tensor(f"{tag}rt", [128, 1], F32))
                sc = S(nc.sbuf_tensor(f"{tag}sc", [1, 1], F32))
                psS = [S(nc.psum_tensor(f"{tag}psS{i}", [128, TT], F32))
                       for i in range(2)]
                psM = [S(nc.psum_tensor(f"{tag}psM{i}", [128, TT], F32))
                       for i in range(2)]
                psA = [S(nc.psum_tensor(f"{tag}psA{i}", [128, TT], F32))
                       for i in range(2)]
                psB = [S(nc.psum_tensor(f"{tag}psB{i}", [128, TT], F32))
                       for i in range(2)]
                lu = [POOL[0], POOL[1]]
                srl = POOL[2]
                sv = POOL[3]
                sp = POOL[4]
                sa = POOL[5]
                so = POOL[6]
                vf_s = POOL[7]
                af_s = POOL[8]

                V_INIT = 1
                V_ROW = lambda t: 2 + 3 * t
                V_VAR = lambda t: 3 + 3 * t
                V_MX = lambda t: 4 + 3 * t
                V_FIN = V_MX(NTT - 1)
                V_SC = V_FIN + 1
                A_SQ = lambda t, d: 9 * t + d + 1
                A_RST = lambda t: 9 * t + 9
                P_S1 = lambda t: 10 * t + 1
                P_SQ = lambda t, d: 10 * t + 2 + d
                P_S2 = lambda t: 10 * t + 9
                P_BC = lambda t: 10 * t + 10
                P_FIN = 10 * NTT + 1

                @blk.sync
                def _(sync):
                    sync = EW(sync)
                    sync.wait_ge(bar, NB0)
                    for t in range(NTT):
                        if t >= 2:
                            sync.wait_ge(sv, V_MX(t - 2))
                        sync.dma_start(
                            out=ub[t % 2][:].rearrange("p (c t) -> p c t", c=DC),
                            in_=src_ap(t).rearrange("(c p) t -> p c t", p=128),
                        ).then_inc(lu[t % 2], 16)
                    sync.wait_ge(sv, V_SC)
                    sync.dma_start(out=mloc_out, in_=sc[:]).then_inc(so, 16)
                    sync.wait_ge(so, 16)
                    sync.wait_ge(srl, 16 * 2 * NTT)
                    sync.dma_start(out=gbar_t, in_=gds[:]).then_inc(bar, 16)

                @blk.gpsimd
                def _(gpsimd):
                    gpsimd = EW(gpsimd)
                    gpsimd.wait_ge(bar, NB0)
                    gpsimd.memset(gdg[:], 0.0).then_inc(bar, 1)

                @blk.tensor
                def _(tensor):
                    tensor = EW(tensor)
                    tensor.wait_ge(bar, NB0)
                    for t in range(NTT):
                        tensor.wait_ge(lu[t % 2], 16 * (t // 2 + 1))
                        if t >= 2:
                            tensor.wait_ge(sv, V_MX(t - 2))
                        for d in range(DC):
                            ds = slice(d * TT, (d + 1) * TT)
                            ins = tensor.matmul(out=psS[t % 2][0:1, :],
                                                lhsT=onecg[:],
                                                rhs=ub[t % 2][:, ds],
                                                start=(d == 0),
                                                stop=(d == DC - 1))
                        ins.then_inc(sp, 1)
                        for d in range(DC):
                            tensor.wait_ge(sa, A_SQ(t, d))
                            tensor.matmul(out=psM[t % 2][0:1, :], lhsT=onecg[:],
                                          rhs=sqt[(t * DC + d) % 2][:],
                                          start=(d == 0), stop=(d == DC - 1)
                                          ).then_inc(sp, 1)
                        tensor.wait_ge(sa, A_RST(t))
                        tensor.matmul(out=psA[t % 2][:], lhsT=onerg[:], rhs=mr[:],
                                      start=True, stop=True)
                        tensor.matmul(out=psB[t % 2][:], lhsT=onerg[:], rhs=rr[:],
                                      start=True, stop=True).then_inc(sp, 1)
                    tensor.wait_ge(sv, V_FIN)
                    tensor.transpose(out=psS[0][0:1, 0:128], in_=mxg[:],
                                     identity=eyeg[:]).then_inc(sp, 1)

                @blk.scalar
                def _(scalar):
                    scalar = EW(scalar)
                    scalar.wait_ge(bar, NB0)
                    af = FE(scalar, af_s)
                    for t in range(NTT):
                        ts_ = slice(t * TT, (t + 1) * TT)
                        scalar.wait_ge(lu[t % 2], 16 * (t // 2 + 1))
                        if t >= 1:
                            scalar.wait_ge(srl, 32 * t)
                        for d in range(DC):
                            ds = slice(d * TT, (d + 1) * TT)
                            g = t * DC + d
                            if g >= 2:
                                scalar.wait_ge(sp, P_SQ((g - 2) // DC,
                                                        (g - 2) % DC))
                            scalar.activation(out=sqt[g % 2][:],
                                              in_=ub[t % 2][:, ds],
                                              func=AF.Square).then_inc(sa, 1)
                        scalar.wait_ge(sv, V_VAR(t))
                        if t >= 1:
                            scalar.wait_ge(sa, A_RST(t - 1))
                        af(scalar.activation(out=lnt[:], in_=vr[:], func=AF.Ln))
                        scalar.activation(out=rr[:], in_=lnt[:], func=AF.Exp,
                                          scale=-0.5).then_inc(sa, 1)
                        scalar.wait_ge(sa, A_RST(t))
                        scalar.dma_start(out=stats_t[0:1, ts_], in_=mr[:]
                                         ).then_inc(srl, 16)
                        scalar.dma_start(out=stats_t[1:2, ts_], in_=rr[:]
                                         ).then_inc(srl, 16)
                    scalar.activation(out=gda[:], in_=gds[:],
                                      func=AF.Identity).then_inc(bar, 1)

                @blk.vector
                def _(vector):
                    vector = EW(vector)
                    vector.wait_ge(bar, NB0)
                    vf = FE(vector, vf_s)
                    vector.memset(mxg[:], 0.0).then_inc(sv, 1)
                    vector.wait_ge(sv, V_INIT)
                    for t in range(NTT):
                        vector.wait_ge(sp, P_S1(t))
                        if t >= 1:
                            vector.wait_ge(srl, 32 * t)
                        vf(vector.tensor_scalar(out=mr[:], in0=psS[t % 2][0:1, :],
                                                scalar1=INV_D, scalar2=None,
                                                op0=ALU.mult))
                        vector.wait_ge(sp, P_S2(t))
                        vector.tensor_scalar(out=t0[:], in0=psM[t % 2][0:1, :],
                                             scalar1=INV_D, scalar2=None,
                                             op0=ALU.mult).then_inc(sv, 1)
                        vf(vector.tensor_tensor(out=t1[:], in0=mr[:], in1=mr[:],
                                                op=ALU.mult))
                        vector.scalar_tensor_tensor(out=vr[:], in0=t0[:],
                                                    scalar=EPS, in1=t1[:],
                                                    op0=ALU.add,
                                                    op1=ALU.subtract
                                                    ).then_inc(sv, 1)
                        vector.wait_ge(sv, V_VAR(t))
                        vector.wait_ge(sp, P_BC(t))
                        for d in range(DC):
                            ds = slice(d * TT, (d + 1) * TT)
                            ins = vector.tensor_sub(out=tmp[:, ds],
                                                    in0=ub[t % 2][:, ds],
                                                    in1=psA[t % 2][:])
                        vf(ins)
                        for d in range(DC):
                            ds = slice(d * TT, (d + 1) * TT)
                            ins = vector.tensor_tensor(out=tmp[:, ds],
                                                       in0=tmp[:, ds],
                                                       in1=psB[t % 2][:],
                                                       op=ALU.mult)
                        vf(ins)
                        for d in range(DC):
                            ds = slice(d * TT, (d + 1) * TT)
                            ins = vector.tensor_reduce(out=red8[:, d:d + 1],
                                                       in_=tmp[:, ds], axis=AX.X,
                                                       op=ALU.max,
                                                       apply_absolute_value=True)
                        vf(ins)
                        vf(vector.tensor_reduce(out=rt[:], in_=red8[:], axis=AX.X,
                                                op=ALU.max,
                                                apply_absolute_value=True))
                        vector.tensor_max(out=mxg[:], in0=mxg[:], in1=rt[:]
                                          ).then_inc(sv, 1)
                        vector.wait_ge(sv, V_MX(t))
                    vector.wait_ge(sp, P_FIN)
                    vector.tensor_reduce(out=sc[:], in_=psS[0][0:1, 0:128],
                                         axis=AX.X, op=ALU.max,
                                         apply_absolute_value=True
                                         ).then_inc(sv, 1)
                    vector.wait_ge(sv, V_SC)
                    vector.memset(gdv[:], 0.0).then_inc(bar, 1)
                @blk.gpsimd
                def _(gpsimd):
                    gpsimd = EW(gpsimd)
                    gpsimd.wait_ge(bar, NB0 + 19)
            bump()

        def emit_combine(site, copies=()):
            NB0 = nbv[0]
            with ExitStack() as st:
                S = st.enter_context
                sc1 = S(nc.sbuf_tensor(f"c{site}sc", [1, 1], F32))
                fone = S(nc.sbuf_tensor(f"c{site}f1", [1, 1], I32))
                flp = S(nc.sbuf_tensor(f"c{site}fp", [1, 2], I32))
                pmaxf = S(nc.sbuf_tensor(f"c{site}pm", [1, 1], F32))
                pmaxg = S(nc.sbuf_tensor(f"c{site}pg", [1, 1], F32))
                cmax = S(nc.sbuf_tensor(f"c{site}cm", [1, 1], F32))
                mallsb = S(nc.sbuf_tensor(f"c{site}mb", [128, 1], F32))
                ld = GPOOL[0]
                wr = POOL[1]
                po = PODED[site]
                pv = GPOOL[1]
                cv = POOL[3]
                sq2 = GPOOL[2]
                cpy = GPOOL[3]

                @blk.vector
                def _(vector):
                    vector = EW(vector)
                    vector.wait_ge(bar, NB0)
                    vector.memset(fone[:], MAGICK)
                    vector.memset(flp[:], 0).then_inc(wr, 1)
                    vector.wait_ge(pv, 32)
                    vector.tensor_tensor(out=cmax[:], in0=pmaxf[:], in1=pmaxg[:],
                                         op=ALU.max).then_inc(cv, 1)
                    vector.wait_ge(cv, 1)
                    vector.tensor_tensor(out=cmax[:], in0=cmax[:], in1=sc1[:],
                                         op=ALU.max).then_inc(cv, 1)
                    vector.wait_ge(cv, 2)
                    vector.memset(gdv[:], 0.0).then_inc(bar, 1)

                @blk.gpsimd
                def _(gpsimd):
                    gpsimd = EW(gpsimd)
                    gpsimd.wait_ge(bar, NB0)
                    ncp = 0
                    for dst_t, src_t, nrows, okey in copies:
                        ncp += 16
                        gpsimd.dma_start(
                            out=dst_t[0:nrows, :],
                            in_=src_t[bass.ds(ids[okey], nrows), :],
                        ).then_inc(cpy, 16)
                    gpsimd.wait_ge(wr, 1)
                    gpsimd.dma_start(out=sc1[:], in_=mloc_t[site]).then_inc(ld, 16)
                    gpsimd.wait_ge(ld, 16)
                    my = ids["my"]
                    gpsimd.dma_start(out=shm_t[bass.ts(my, 1), site:site + 1],
                                     in_=sc1[:].bitcast(I32)).then_inc(ld, 16)
                    gpsimd.wait_ge(ld, 32)
                    gpsimd.dma_start(out=flg_t[bass.ts(my, 1), site:site + 1],
                                     in_=fone[:]).then_inc(ld, 16)
                    gpsimd.wait_ge(ld, 48)
                    with gpsimd.register() as rcnt, gpsimd.register() as rf, \
                         gpsimd.register() as rc:
                        gpsimd.reg_mov(rcnt, 0)
                        for row in range(2):

                            def cond(row=row):
                                gpsimd.reg_load(rf, flp[0:1, row:row + 1])
                                gpsimd.reg_alu(rc, rf, MAGICK, ALU.is_lt)
                                return rc

                            with gpsimd.While(cond):
                                gpsimd.reg_add(rcnt, rcnt, 16)
                                gpsimd.dma_start(
                                    out=flp[0:1, row:row + 1],
                                    in_=flg_t[row:row + 1, site:site + 1]
                                ).then_inc(po, 16)
                                gpsimd.wait_ge(po, rcnt)
                    gpsimd.dma_start(out=pmaxf[:].bitcast(I32),
                                     in_=shm_t[0:1, site:site + 1]
                                     ).then_inc(pv, 16)
                    gpsimd.wait_ge(pv, 16)
                    gpsimd.dma_start(out=pmaxg[:].bitcast(I32),
                                     in_=shm_t[1:2, site:site + 1]
                                     ).then_inc(pv, 16)
                    gpsimd.wait_ge(pv, 32)
                    gpsimd.wait_ge(cv, 2)
                    gpsimd.dma_start(out=cmd_t[site], in_=cmax[:]).then_inc(sq2, 16)
                    gpsimd.wait_ge(sq2, 16)
                    gpsimd.dma_start(out=mallsb[:],
                                     in_=cmd_t[site].broadcast_to([128, 1])
                                     ).then_inc(sq2, 16)
                    gpsimd.wait_ge(sq2, 32)
                    gpsimd.dma_start(out=mall_t[site], in_=mallsb[:]
                                     ).then_inc(sq2, 16)
                    gpsimd.wait_ge(sq2, 48)
                    gpsimd.wait_ge(cpy, ncp)
                    gpsimd.memset(gdg[:], 0.0).then_inc(bar, 1)

                @blk.scalar
                def _(scalar):
                    scalar = EW(scalar)
                    scalar.wait_ge(bar, NB0)
                    scalar.activation(out=gda[:], in_=gds[:],
                                      func=AF.Identity).then_inc(bar, 1)

                @blk.sync
                def _(sync):
                    sync = EW(sync)
                    sync.wait_ge(bar, NB0)
                    sync.dma_start(out=gbar_t, in_=gds[:]).then_inc(bar, 16)
                @blk.gpsimd
                def _(gpsimd):
                    gpsimd = EW(gpsimd)
                    gpsimd.wait_ge(bar, NB0 + 19)
            bump()

        def emit_ffn_in(tag, b, src_ap, stats_t, mall_in, hdst_t, prequant,
                        zero_gmx, final, mloc_out):
            NB0 = nbv[0]
            B0 = b * TB
            hrow0 = b * H
            NG = HC * NT
            NB_ = NG // 4
            with ExitStack() as st:
                S = st.enter_context
                mrow_t2 = S(nc.sbuf_tensor(f"{tag}mrow", [1, TB], F32))
                rrow_t2 = S(nc.sbuf_tensor(f"{tag}rrow", [1, TB], F32))
                mrow = mrow_t2[0:1, :]
                rrow = rrow_t2[0:1, :]
                ubuf = [S(nc.sbuf_tensor(f"{tag}ub{i}", [128, TB], F32))
                        for i in range(2)]
                wbuf = [S(nc.sbuf_tensor(f"{tag}wb{i}", [128, H], F32))
                        for i in range(2)]
                xq = [S(nc.sbuf_tensor(f"{tag}xq{d}", [128, TB], BF16))
                      for d in range(DC)]
                wqs = [S(nc.sbuf_tensor(f"{tag}wqs{i}", [128, DC * 128], BF16))
                       for i in range(2)]
                wqo = [S(nc.sbuf_tensor(f"{tag}wqo{i}", [128, H], BF16))
                       for i in range(2)]
                wint = S(nc.sbuf_tensor(f"{tag}wint", [128, H // 4], I32))
                xint = S(nc.sbuf_tensor(f"{tag}xint", [128, TT], I32))
                qf = S(nc.sbuf_tensor(f"{tag}qf", [128, TT], F32))
                red8 = S(nc.sbuf_tensor(f"{tag}red8", [128, DC], F32))
                rt2 = S(nc.sbuf_tensor(f"{tag}rt2", [128, 1], F32))
                mallb = S(nc.sbuf_tensor(f"{tag}mallb", [128, 1], F32))
                b1c = S(nc.sbuf_tensor(f"{tag}b1c", [128, HC], F32))
                b1s = S(nc.sbuf_tensor(f"{tag}b1s", [128, HC], F32))
                tmpq = S(nc.sbuf_tensor(f"{tag}tmpq", [128, TB], F32))
                wmx = S(nc.sbuf_tensor(f"{tag}wmx", [128, 1], F32))
                mxm = S(nc.sbuf_tensor(f"{tag}mxm", [128, 1], F32))
                sxb = S(nc.sbuf_tensor(f"{tag}sxb", [128, 1], F32))
                rxb = S(nc.sbuf_tensor(f"{tag}rxb", [128, 1], F32))
                s1b = S(nc.sbuf_tensor(f"{tag}s1b", [128, 1], F32))
                swb = S(nc.sbuf_tensor(f"{tag}swb", [128, 1], F32))
                rwb = S(nc.sbuf_tensor(f"{tag}rwb", [128, 1], F32))
                sc1 = S(nc.sbuf_tensor(f"{tag}sc1", [128, 1], F32))
                dmy = S(nc.sbuf_tensor(f"{tag}dmy", [128, 1], F32))
                dmy2 = S(nc.sbuf_tensor(f"{tag}dmy2", [128, 1], F32))
                wsc = S(nc.sbuf_tensor(f"{tag}wsc", [1, 1], F32))
                sw_s = S(nc.sbuf_tensor(f"{tag}sws", [1, 1], F32))
                rw1 = S(nc.sbuf_tensor(f"{tag}rw1", [1, 1], F32))
                r1a = S(nc.sbuf_tensor(f"{tag}r1a", [1, 1], F32))
                srow = S(nc.sbuf_tensor(f"{tag}srow", [1, 2], F32))
                mgl = S(nc.sbuf_tensor(f"{tag}mgl", [1, 1], F32))
                pmm = [S(nc.psum_tensor(f"{tag}pmm{i}", [128, TT], F32))
                       for i in range(4)]
                psE = S(nc.psum_tensor(f"{tag}psE", [128, TT], F32))
                psF = S(nc.psum_tensor(f"{tag}psF", [128, TT], F32))
                paux = S(nc.psum_tensor(f"{tag}paux", [128, 128], F32))
                pbc = S(nc.psum_tensor(f"{tag}pbc", [128, 2], F32))
                lfix = POOL[0]
                lu = [POOL[1], POOL[2]]
                gw = [GPOOL[0], GPOOL[1]]
                gq = [GPOOL[2], GPOOL[3]]
                sq = [POOL[7], POOL[8]]
                sh = [POOL[9], POOL[10]]
                sws_sem = POOL[11]
                sv = POOL[12]
                sp = POOL[13]
                sa = POOL[14]
                so = POOL[15]
                vf_s = POOL[16]

                V_ONES = 1
                V_WCH = lambda d: 2 + d
                V_WMX = 10
                V_SROW = 11
                V_SCAL = 12
                V_WQ = lambda d: 13 + d
                V_XQ = lambda t: 21 + t
                V_G = lambda g: 29 + g
                V_GDONE = 29 + NG
                V_FIN = 30 + NG
                P_TR = 1
                P_BC = 2
                P_B = lambda t: 3 + t
                P_G = lambda g: 11 + g
                P_FINT = 11 + NG

                @blk.sync
                def _(sync):
                    sync = EW(sync)
                    sync.wait_ge(bar, NB0)
                    sync.dma_start(out=mrow_t2[:], in_=stats_t[0:1, B0:B0 + TB]
                                   ).then_inc(lfix, 16)
                    sync.dma_start(out=rrow_t2[:], in_=stats_t[1:2, B0:B0 + TB]
                                   ).then_inc(lfix, 16)
                    sync.dma_start(out=mallb[:], in_=mall_in).then_inc(lfix, 16)
                    sync.dma_start(out=b1c[:], in_=b1c_t).then_inc(lfix, 16)
                    if prequant:
                        sync.dma_start(out=swb[:], in_=swd1_t).then_inc(lfix, 16)
                    for t in range(NT):
                        if t >= 2:
                            sync.wait_ge(sv, V_XQ(t - 2))
                        sync.dma_start(
                            out=ubuf[t % 2][:].rearrange("p (c t) -> p c t", c=DC),
                            in_=src_ap(t).rearrange("(c p) t -> p c t", p=128),
                        ).then_inc(lu[t % 2], 16)
                    if final:
                        sync.wait_ge(sv, V_FIN)
                        sync.dma_start(out=mloc_out, in_=mgl[:]).then_inc(so, 16)
                        sync.wait_ge(so, 16)
                    else:
                        sync.wait_ge(sv, V_GDONE)
                    if not prequant:
                        sync.wait_ge(sq[0], 16 * (DC // 2))
                        sync.wait_ge(sq[1], 16 * (DC // 2))
                        sync.wait_ge(sws_sem, 16)
                    sync.wait_ge(sh[0], 16 * (NB_ // 2))
                    sync.wait_ge(sh[1], 16 * (NB_ // 2))
                    sync.dma_start(out=gbar_t, in_=gds[:]).then_inc(bar, 16)

                @blk.gpsimd
                def _(gpsimd):
                    gpsimd = EW(gpsimd)
                    gpsimd.wait_ge(bar, NB0)
                    if not prequant:
                        for d in range(DC):
                            if d >= 2:
                                gpsimd.wait_ge(sv, V_WCH(d - 2))
                            gpsimd.dma_start(out=wbuf[d % 2][:],
                                             in_=w1T_t[d * 128:(d + 1) * 128, :]
                                             ).then_inc(gw[d % 2], 16)
                        gpsimd.wait_ge(sv, V_WCH(DC - 1))
                        for d in range(DC):
                            if d >= 2:
                                gpsimd.wait_ge(sv, V_WQ(d - 2))
                            gpsimd.dma_start(out=wbuf[d % 2][:],
                                             in_=w1T_t[d * 128:(d + 1) * 128, :]
                                             ).then_inc(gw[d % 2], 16)
                        gpsimd.wait_ge(sq[0], 16 * (DC // 2))
                        gpsimd.wait_ge(sq[1], 16 * (DC // 2))
                    for h in range(HC):
                        if h >= 2:
                            gpsimd.wait_ge(sp, P_G((h - 1) * NT - 1))
                        gpsimd.dma_start(
                            out=wqs[h % 2][:].rearrange("p (c m) -> p c m", c=DC),
                            in_=wq1_t[:, h * 128:(h + 1) * 128].rearrange(
                                "(c p) m -> p c m", p=128),
                        ).then_inc(gq[h % 2], 16)
                    gpsimd.memset(gdg[:], 0.0).then_inc(bar, 1)

                @blk.vector
                def _(vector):
                    vector = EW(vector)
                    vector.wait_ge(bar, NB0)
                    vf = FE(vector, vf_s)
                    vf(vector.memset(dmy2[:], 0.0))
                    vector.memset(wmx[:], 0.0)
                    if zero_gmx:
                        vector.memset(gmxg[:], 0.0).then_inc(sv, 1)
                    else:
                        vector.memset(dmy[:], 0.0).then_inc(sv, 1)
                    vector.wait_ge(sv, V_ONES)
                    vector.wait_ge(lfix, 16 * (5 if prequant else 4))
                    vf(vector.tensor_reduce(out=mxm[:], in_=mallb[:], axis=AX.X,
                                            op=ALU.max, apply_absolute_value=True))
                    vf(vector.tensor_scalar(out=sxb[:], in0=mxm[:], scalar1=MINR,
                                            scalar2=1.0 / QP, op0=ALU.max,
                                            op1=ALU.mult))
                    recip_newton_f(vector, vf, rxb[:], sxb[:], sc1[:])
                    vf(vector.tensor_scalar(out=rrow, in0=rrow,
                                            scalar1=rxb[0:1, 0:1],
                                            scalar2=None, op0=ALU.mult))
                    if not prequant:
                        for d in range(DC):
                            vector.wait_ge(gw[d % 2], 16 * (d // 2 + 1))
                            vector.tensor_reduce(out=red8[:, d:d + 1],
                                                 in_=wbuf[d % 2][:], axis=AX.X,
                                                 op=ALU.max,
                                                 apply_absolute_value=True
                                                 ).then_inc(sv, 1)
                        vector.wait_ge(sv, V_WCH(DC - 1))
                        vector.tensor_reduce(out=wmx[:], in_=red8[:], axis=AX.X,
                                             op=ALU.max, apply_absolute_value=True
                                             ).then_inc(sv, 1)
                        vector.wait_ge(sv, V_WMX)
                        vector.wait_ge(sp, P_TR)
                        vf(vector.tensor_reduce(out=wsc[:], in_=paux[0:1, 0:128],
                                                axis=AX.X, op=ALU.max,
                                                apply_absolute_value=True))
                        vf(vector.tensor_scalar(out=sw_s[:], in0=wsc[:],
                                                scalar1=MINR, scalar2=1.0 / QP,
                                                op0=ALU.max, op1=ALU.mult))
                    else:
                        for _k in range(DC + 1):
                            vector.tensor_copy(out=dmy[:], in_=sxb[:]
                                               ).then_inc(sv, 1)
                            vector.wait_ge(sv, V_ONES + 1 + _k)
                        vector.wait_ge(sv, V_WMX)
                    if prequant:
                        recip_newton_f(vector, vf, rwb[:], swb[:], sc1[:])
                        vector.tensor_copy(out=dmy[:], in_=rwb[:]
                                           ).then_inc(sv, 1)
                        vector.wait_ge(sv, V_SROW)
                        vf(vector.tensor_tensor(out=s1b[:], in0=swb[:],
                                                in1=sxb[:], op=ALU.mult))
                        vector.tensor_scalar(out=b1s[:], in0=b1c[:],
                                             scalar1=s1b[:, 0:1], scalar2=None,
                                             op0=ALU.mult).then_inc(sv, 1)
                    else:
                        recip_newton_f(vector, vf, rw1[:], sw_s[:], r1a[:])
                        vector.tensor_copy(out=srow[0:1, 0:1], in_=sw_s[:])
                        vector.tensor_copy(out=srow[0:1, 1:2], in_=rw1[:]
                                           ).then_inc(sv, 1)
                        vector.wait_ge(sp, P_BC)
                        vector.tensor_copy(out=swb[:], in_=pbc[:, 0:1])
                        vf(vector.tensor_copy(out=rwb[:], in_=pbc[:, 1:2]))
                        vf(vector.tensor_tensor(out=s1b[:], in0=swb[:],
                                                in1=sxb[:], op=ALU.mult))
                        vector.tensor_scalar(out=b1s[:], in0=b1c[:],
                                             scalar1=s1b[:, 0:1], scalar2=None,
                                             op0=ALU.mult).then_inc(sv, 1)
                    for d in range(DC):
                        if prequant:
                            vector.tensor_copy(out=dmy[:], in_=rwb[:]
                                               ).then_inc(sv, 1)
                            vector.wait_ge(sv, V_WQ(d))
                            continue
                        vector.wait_ge(gw[d % 2], 16 * (4 + d // 2 + 1))
                        if d >= 2:
                            vector.wait_ge(sq[d % 2], 16 * (d // 2))
                        for k in range(4):
                            hs = slice(k * (H // 4), (k + 1) * (H // 4))
                            wf = tmpq[:, 0:H // 4]
                            vf(vector.tensor_scalar(out=wf, in0=wbuf[d % 2][:, hs],
                                                    scalar1=rwb[:, 0:1],
                                                    scalar2=MAGIC,
                                                    op0=ALU.mult, op1=ALU.add))
                            vf(vector.tensor_scalar(out=wint[:], in0=wf,
                                                    scalar1=MAGIC, scalar2=None,
                                                    op0=ALU.subtract))
                            ins = vector.tensor_copy(out=wqo[d % 2][:, hs],
                                                     in_=wint[:])
                            if k < 3:
                                vf(ins)
                        ins.then_inc(sv, 1)
                        vector.wait_ge(sv, V_WQ(d))
                    for t in range(NT):
                        ts_ = slice(t * TT, (t + 1) * TT)
                        vector.wait_ge(lu[t % 2], 16 * (t // 2 + 1))
                        vector.wait_ge(sp, P_B(t))
                        for d in range(DC):
                            ds = slice(d * TT, (d + 1) * TT)
                            ins = vector.tensor_sub(out=tmpq[:, ds],
                                                    in0=ubuf[t % 2][:, ds],
                                                    in1=psE[:])
                        vf(ins)
                        for d in range(DC):
                            ds = slice(d * TT, (d + 1) * TT)
                            vf(vector.tensor_tensor(out=qf[:], in0=tmpq[:, ds],
                                                    in1=psF[:], op=ALU.mult))
                            vf(vector.tensor_scalar(out=xint[:], in0=qf[:],
                                                    scalar1=MAGIC, scalar2=MAGIC,
                                                    op0=ALU.add,
                                                    op1=ALU.subtract))
                            ins = vector.tensor_copy(out=xq[d][:, ts_],
                                                     in_=xint[:])
                            if d < DC - 1:
                                vf(ins)
                        ins.then_inc(sv, 1)
                        vector.wait_ge(sv, V_XQ(t))
                    for g in range(NG):
                        vector.wait_ge(sa, g + 1)
                        stg = wbuf[(g // 4) % 2]
                        col = (g % 4) * TT
                        vector.tensor_reduce(out=red8[:, g % 8:g % 8 + 1],
                                             in_=stg[:, col:col + TT], axis=AX.X,
                                             op=ALU.max, apply_absolute_value=True
                                             ).then_inc(sv, 1)
                        if g % 8 == 7:
                            vector.wait_ge(sv, V_G(g))
                            vf(vector.tensor_reduce(out=rt2[:], in_=red8[:],
                                                    axis=AX.X, op=ALU.max,
                                                    apply_absolute_value=True))
                            if g == NG - 1:
                                vector.tensor_max(out=gmxg[:], in0=gmxg[:],
                                                  in1=rt2[:]).then_inc(sv, 1)
                            else:
                                vf(vector.tensor_max(out=gmxg[:], in0=gmxg[:],
                                                     in1=rt2[:]))
                    if final:
                        vector.wait_ge(sp, P_FINT)
                        vector.tensor_reduce(out=mgl[:], in_=paux[0:1, 0:128],
                                             axis=AX.X, op=ALU.max,
                                             apply_absolute_value=True
                                             ).then_inc(sv, 1)
                        vector.wait_ge(sv, V_FIN)
                    vector.memset(gdv[:], 0.0).then_inc(bar, 1)

                @blk.tensor
                def _(tensor):
                    tensor = EW(tensor)
                    tensor.wait_ge(bar, NB0)
                    tensor.wait_ge(lfix, 16 * (5 if prequant else 4))
                    if not prequant:
                        tensor.wait_ge(sv, V_WMX)
                        tensor.transpose(out=paux[0:1, 0:128], in_=wmx[:],
                                         identity=eyeg[:]).then_inc(sp, 1)
                        tensor.wait_ge(sv, V_SROW)
                        tensor.matmul(out=pbc[:], lhsT=onerg[:], rhs=srow[:],
                                      start=True, stop=True).then_inc(sp, 1)
                    else:
                        tensor.wait_ge(sv, V_ONES)
                        tensor.matmul(out=pbc[:, 0:1], lhsT=onerg[:],
                                      rhs=onerg[0:1, 0:1], start=True, stop=True
                                      ).then_inc(sp, 1)
                        tensor.wait_ge(sv, V_SROW)
                        tensor.matmul(out=pbc[:, 0:1], lhsT=onerg[:],
                                      rhs=onerg[0:1, 0:1], start=True, stop=True
                                      ).then_inc(sp, 1)
                    for t in range(NT):
                        ts_ = slice(t * TT, (t + 1) * TT)
                        if t >= 1:
                            tensor.wait_ge(sv, V_XQ(t - 1))
                        tensor.matmul(out=psE[:], lhsT=onerg[:],
                                      rhs=mrow[:, ts_], start=True, stop=True)
                        tensor.matmul(out=psF[:], lhsT=onerg[:],
                                      rhs=rrow[:, ts_], start=True, stop=True
                                      ).then_inc(sp, 1)
                    g = 0
                    for h in range(HC):
                        tensor.wait_ge(gq[h % 2], 16 * (h // 2 + 1))
                        for t in range(NT):
                            ts_ = slice(t * TT, (t + 1) * TT)
                            if h == 0:
                                tensor.wait_ge(sv, V_XQ(t))
                            if g >= 4:
                                tensor.wait_ge(sa, g - 3)
                            for d in range(DC):
                                ins = tensor.matmul(
                                    out=pmm[g % 4][:],
                                    lhsT=wqs[h % 2][:, d * 128:(d + 1) * 128],
                                    rhs=xq[d][:, ts_], start=(d == 0),
                                    stop=(d == DC - 1))
                            ins.then_inc(sp, 1)
                            g += 1
                    if final:
                        tensor.wait_ge(sv, V_GDONE)
                        tensor.transpose(out=paux[0:1, 0:128], in_=gmxg[:],
                                         identity=eyeg[:]).then_inc(sp, 1)

                @blk.scalar
                def _(scalar):
                    scalar = EW(scalar)
                    scalar.wait_ge(bar, NB0)
                    if not prequant:
                        scalar.wait_ge(sv, V_SCAL)
                        scalar.dma_start(out=swd1_t, in_=swb[:]
                                         ).then_inc(sws_sem, 16)
                        for d in range(DC):
                            scalar.wait_ge(sv, V_WQ(d))
                            scalar.dma_start(out=wq1_t[d * 128:(d + 1) * 128, :],
                                             in_=wqo[d % 2][:]
                                             ).then_inc(sq[d % 2], 16)
                    scalar.wait_ge(sv, V_SCAL)
                    scalar.wait_ge(sv, V_WQ(DC - 1))
                    g = 0
                    for h in range(HC):
                        for t in range(NT):
                            bb = g // 4
                            stg = wbuf[bb % 2]
                            col = (g % 4) * TT
                            if g % 4 == 0 and bb >= 2:
                                scalar.wait_ge(sh[bb % 2], 16 * (bb // 2))
                                scalar.wait_ge(sv, V_G(4 * (bb - 1) - 1))
                            scalar.wait_ge(sp, P_G(g))
                            scalar.activation(out=stg[:, col:col + TT],
                                              in_=pmm[g % 4][:], func=AF.Gelu,
                                              bias=b1s[:, h:h + 1],
                                              scale=s1b[:, 0:1]).then_inc(sa, 1)
                            if g % 4 == 3:
                                scalar.wait_ge(sa, g + 1)
                                th = (t - 3) // 4
                                scalar.dma_start(
                                    out=hdst_t[hrow0 + h * 128:hrow0 + (h + 1) * 128,
                                               th * 2048:(th + 1) * 2048],
                                    in_=stg[:, 0:2048]).then_inc(sh[bb % 2], 16)
                            g += 1
                    scalar.activation(out=gda[:], in_=gds[:],
                                      func=AF.Identity).then_inc(bar, 1)
                @blk.gpsimd
                def _(gpsimd):
                    gpsimd = EW(gpsimd)
                    gpsimd.wait_ge(bar, NB0 + 19)
            bump()

        def emit_ffn_out(tag, b, src_t, h_t, mall_in, dst_t, prequant,
                         hrow0=None, drow0=0):
            NB0 = nbv[0]
            B0 = b * TB
            if hrow0 is None:
                hrow0 = b * H
            NW = HC
            NQ = 4
            with ExitStack() as st:
                S = st.enter_context
                wbuf = [S(nc.sbuf_tensor(f"{tag}wb{i}", [128, D], F32))
                        for i in range(2)]
                wq2g = [S(nc.sbuf_tensor(f"{tag}wqg{i}", [128, 4 * D], BF16))
                        for i in range(NW // 4)]
                wq2r = [wq2g[h // 4][:, (h % 4) * D:(h % 4 + 1) * D]
                        for h in range(NW)]
                hbuf = [S(nc.sbuf_tensor(f"{tag}hb{i}", [128, DC * TT], F32))
                        for i in range(2)]
                hq = [S(nc.sbuf_tensor(f"{tag}hq{i}", [128, DC * TT], BF16))
                      for i in range(2)]
                ubuf = [S(nc.sbuf_tensor(f"{tag}ub{i}", [128, TB], F32))
                        for i in range(2)]
                xo = [S(nc.sbuf_tensor(f"{tag}xo{i}", [128, TB], F32))
                      for i in range(2)]
                wint = S(nc.sbuf_tensor(f"{tag}wint", [128, D], I32))
                wf = S(nc.sbuf_tensor(f"{tag}wf", [128, D], F32))
                xint = S(nc.sbuf_tensor(f"{tag}xint", [128, TT], I32))
                qf = S(nc.sbuf_tensor(f"{tag}qf", [128, TT], F32))
                mallb = S(nc.sbuf_tensor(f"{tag}mallb", [128, 1], F32))
                b2c = S(nc.sbuf_tensor(f"{tag}b2c", [128, DC], F32))
                b2s = S(nc.sbuf_tensor(f"{tag}b2s", [128, DC], F32))
                red8 = S(nc.sbuf_tensor(f"{tag}red8", [128, NW], F32))
                wmx = S(nc.sbuf_tensor(f"{tag}wmx", [128, 1], F32))
                mxm = S(nc.sbuf_tensor(f"{tag}mxm", [128, 1], F32))
                sxb = S(nc.sbuf_tensor(f"{tag}sxb", [128, 1], F32))
                rxb = S(nc.sbuf_tensor(f"{tag}rxb", [128, 1], F32))
                s5b = S(nc.sbuf_tensor(f"{tag}s5b", [128, 1], F32))
                swb = S(nc.sbuf_tensor(f"{tag}swb", [128, 1], F32))
                rwb = S(nc.sbuf_tensor(f"{tag}rwb", [128, 1], F32))
                sc1 = S(nc.sbuf_tensor(f"{tag}sc1", [128, 1], F32))
                dmy = S(nc.sbuf_tensor(f"{tag}dmy", [128, 1], F32))
                wsc = S(nc.sbuf_tensor(f"{tag}wsc", [1, 1], F32))
                sw_s = S(nc.sbuf_tensor(f"{tag}sws", [1, 1], F32))
                rw1 = S(nc.sbuf_tensor(f"{tag}rw1", [1, 1], F32))
                r1a = S(nc.sbuf_tensor(f"{tag}r1a", [1, 1], F32))
                srow = S(nc.sbuf_tensor(f"{tag}srow", [1, 2], F32))
                pmm = [S(nc.psum_tensor(f"{tag}pmm{i}", [128, TT], F32))
                       for i in range(DC)]
                lfix = POOL[0]
                lu = [POOL[1], POOL[2]]
                lh = [GPOOL[2], POOL[4]]
                gw = [GPOOL[0], GPOOL[1]]
                sx_ = [POOL[7], POOL[8]]
                swq = [POOL[9], POOL[10]]
                so_ = POOL[11]
                sv = POOL[12]
                sp = POOL[13]
                sa = POOL[14]
                vf_s = POOL[15]

                V_ONES = 1
                V_WCH = lambda h: 2 + h
                V_WMX = 2 + NW
                V_SROW = V_WMX + 1
                V_SCAL = V_SROW + 1
                V_WQ = lambda h: V_SCAL + 1 + h
                _pos = {}
                _ctr = V_WQ(NW - 1)
                for _t in range(NT):
                    for _q in range(NQ):
                        _ctr += 1
                        _pos[("q", _t, _q)] = _ctr
                    if _t >= 1:
                        _ctr += 1
                        _pos[("a", _t - 1)] = _ctr
                _ctr += 1
                _pos[("a", NT - 1)] = _ctr
                V_HQ = lambda t, q: _pos[("q", t, q)]
                V_XO = lambda t: _pos[("a", t)]
                V_FIN = V_XO(NT - 1)
                V_SWST = V_SROW
                P_TR = 1
                P_BC = 2
                P_MM = lambda t, q: 3 + t * NQ + q
                P_T = lambda t: P_MM(t, NQ - 1)

                @blk.sync
                def _(sync):
                    sync = EW(sync)
                    sync.wait_ge(bar, NB0)
                    sync.dma_start(out=mallb[:], in_=mall_in).then_inc(lfix, 16)
                    sync.dma_start(out=b2c[:], in_=b2c_t).then_inc(lfix, 16)
                    if prequant:
                        sync.dma_start(out=swb[:], in_=swd2_t).then_inc(lfix, 16)
                    for t in range(NT):
                        if t >= 2:
                            sync.wait_ge(sv, V_XO(t - 2))
                        sync.dma_start(
                            out=ubuf[t % 2][:].rearrange("p (c t) -> p c t", c=DC),
                            in_=src_t[:, B0 + t * TT:B0 + (t + 1) * TT].rearrange(
                                "(c p) t -> p c t", p=128),
                        ).then_inc(lu[t % 2], 16)
                        for j in (4 * t + 1, 4 * t + 3):
                            tq, qq = divmod(j, NQ)
                            if j >= 2:
                                sync.wait_ge(sv, V_HQ(*divmod(j - 2, NQ)))
                            sync.dma_start(
                                out=hbuf[1][:].rearrange("p (c t) -> p c t", c=DC),
                                in_=h_t[hrow0 + qq * (H // NQ):
                                        hrow0 + (qq + 1) * (H // NQ),
                                        tq * TT:(tq + 1) * TT
                                        ].rearrange("(c p) t -> p c t", p=128),
                            ).then_inc(lh[1], 16)
                    sync.wait_ge(sv, V_FIN)
                    if not prequant:
                        sync.wait_ge(sv, V_SWST)
                        sync.dma_start(out=swd2_t, in_=swb[:]).then_inc(so_, 16)
                        sync.wait_ge(so_, 16)
                        sync.wait_ge(swq[0], 16 * (NW // 2))
                        sync.wait_ge(swq[1], 16 * (NW // 2))
                    sync.wait_ge(sx_[0], 16 * (NT // 2))
                    sync.wait_ge(sx_[1], 16 * (NT // 2))
                    sync.dma_start(out=gbar_t, in_=gds[:]).then_inc(bar, 16)

                @blk.gpsimd
                def _(gpsimd):
                    gpsimd = EW(gpsimd)
                    gpsimd.wait_ge(bar, NB0)
                    if prequant:
                        for gidx in range(NW // 4):
                            if gidx >= 2:
                                gpsimd.wait_ge(gw[gidx % 2], 16 * (gidx // 2))
                            gpsimd.dma_start(
                                out=wq2g[gidx][:].rearrange("p (c m) -> p c m",
                                                            c=4),
                                in_=wq2_t[gidx * 512:(gidx + 1) * 512, :
                                          ].rearrange("(c p) m -> p c m", p=128),
                            ).then_inc(gw[gidx % 2], 16)
                    else:
                        for h in range(NW):
                            if h >= 2:
                                gpsimd.wait_ge(sv, V_WCH(h - 2))
                            gpsimd.dma_start(out=wbuf[h % 2][:],
                                             in_=w2T_t[h * 128:(h + 1) * 128, :]
                                             ).then_inc(gw[h % 2], 16)
                        gpsimd.wait_ge(sv, V_WCH(NW - 1))
                        for h in range(NW):
                            if h >= 2:
                                gpsimd.wait_ge(sv, V_WQ(h - 2))
                            gpsimd.dma_start(out=wbuf[h % 2][:],
                                             in_=w2T_t[h * 128:(h + 1) * 128, :]
                                             ).then_inc(gw[h % 2], 16)
                    for t in range(NT):
                        for q in range(NQ):
                            j = t * NQ + q
                            if j % 2 != 0:
                                continue
                            if j >= 2:
                                gpsimd.wait_ge(sv, V_HQ(*divmod(j - 2, NQ)))
                            gpsimd.dma_start(
                                out=hbuf[0][:].rearrange("p (c t) -> p c t", c=DC),
                                in_=h_t[hrow0 + q * (H // NQ):
                                        hrow0 + (q + 1) * (H // NQ),
                                        t * TT:(t + 1) * TT
                                        ].rearrange("(c p) t -> p c t", p=128),
                            ).then_inc(lh[0], 16)
                    gpsimd.memset(gdg[:], 0.0).then_inc(bar, 1)

                @blk.vector
                def _(vector):
                    vector = EW(vector)
                    vector.wait_ge(bar, NB0)
                    vf = FE(vector, vf_s)
                    vf(vector.memset(dmy[:], 0.0))
                    vector.memset(wmx[:], 0.0).then_inc(sv, 1)
                    vector.wait_ge(sv, V_ONES)
                    vector.wait_ge(lfix, 16 * (3 if prequant else 2))
                    vf(vector.tensor_reduce(out=mxm[:], in_=mallb[:], axis=AX.X,
                                            op=ALU.max, apply_absolute_value=True))
                    vf(vector.tensor_scalar(out=sxb[:], in0=mxm[:], scalar1=MINR,
                                            scalar2=1.0 / QP, op0=ALU.max,
                                            op1=ALU.mult))
                    recip_newton_f(vector, vf, rxb[:], sxb[:], sc1[:])
                    if not prequant:
                        for h in range(NW):
                            vector.wait_ge(gw[h % 2], 16 * (h // 2 + 1))
                            vector.tensor_reduce(out=red8[:, h:h + 1],
                                                 in_=wbuf[h % 2][:], axis=AX.X,
                                                 op=ALU.max,
                                                 apply_absolute_value=True
                                                 ).then_inc(sv, 1)
                        vector.wait_ge(sv, V_WCH(NW - 1))
                        vector.tensor_reduce(out=wmx[:], in_=red8[:], axis=AX.X,
                                             op=ALU.max, apply_absolute_value=True
                                             ).then_inc(sv, 1)
                        vector.wait_ge(sv, V_WMX)
                        vector.wait_ge(sp, P_TR)
                        vf(vector.tensor_reduce(out=wsc[:], in_=pmm[0][0:1, 0:128],
                                                axis=AX.X, op=ALU.max,
                                                apply_absolute_value=True))
                        vf(vector.tensor_scalar(out=sw_s[:], in0=wsc[:],
                                                scalar1=MINR, scalar2=1.0 / QP,
                                                op0=ALU.max, op1=ALU.mult))
                    else:
                        for _k in range(NW + 1):
                            vector.tensor_copy(out=dmy[:], in_=sxb[:]
                                               ).then_inc(sv, 1)
                            vector.wait_ge(sv, V_ONES + 1 + _k)
                        vector.wait_ge(sv, V_WMX)
                    if prequant:
                        recip_newton_f(vector, vf, rwb[:], swb[:], sc1[:])
                        vector.tensor_copy(out=dmy[:], in_=rwb[:]
                                           ).then_inc(sv, 1)
                        vector.wait_ge(sv, V_SROW)
                    else:
                        recip_newton_f(vector, vf, rw1[:], sw_s[:], r1a[:])
                        vector.tensor_copy(out=srow[0:1, 0:1], in_=sw_s[:])
                        vector.tensor_copy(out=srow[0:1, 1:2], in_=rw1[:]
                                           ).then_inc(sv, 1)
                        vector.wait_ge(sp, P_BC)
                        vector.tensor_copy(out=swb[:], in_=pmm[0][:, 0:1])
                        vf(vector.tensor_copy(out=rwb[:], in_=pmm[0][:, 1:2]))
                    vf(vector.tensor_tensor(out=s5b[:], in0=swb[:], in1=sxb[:],
                                            op=ALU.mult))
                    vf(vector.tensor_scalar(out=s5b[:], in0=s5b[:], scalar1=0.5,
                                            scalar2=None, op0=ALU.mult))
                    vector.tensor_scalar(out=b2s[:], in0=b2c[:],
                                         scalar1=s5b[:, 0:1], scalar2=None,
                                         op0=ALU.mult).then_inc(sv, 1)
                    for h in range(NW):
                        if prequant:
                            gidx = h // 4
                            vector.wait_ge(gw[gidx % 2], 16 * (gidx // 2 + 1))
                            vector.tensor_copy(out=dmy[:], in_=rwb[:]
                                               ).then_inc(sv, 1)
                            vector.wait_ge(sv, V_WQ(h))
                            continue
                        vector.wait_ge(gw[h % 2], 16 * (NW // 2 + h // 2 + 1))
                        vf(vector.tensor_scalar(out=wf[:], in0=wbuf[h % 2][:],
                                                scalar1=rwb[:, 0:1], scalar2=MAGIC,
                                                op0=ALU.mult, op1=ALU.add))
                        vf(vector.tensor_scalar(out=wint[:], in0=wf[:],
                                                scalar1=MAGIC, scalar2=None,
                                                op0=ALU.subtract))
                        vector.tensor_copy(out=wq2r[h], in_=wint[:]
                                           ).then_inc(sv, 1)
                        vector.wait_ge(sv, V_WQ(h))

                    def residual_add(t):
                        vector.wait_ge(lu[t % 2], 16 * (t // 2 + 1))
                        vector.wait_ge(sa, DC * (t + 1))
                        for d in range(DC):
                            ds = slice(d * TT, (d + 1) * TT)
                            ins = vector.tensor_add(out=xo[t % 2][:, ds],
                                                    in0=xo[t % 2][:, ds],
                                                    in1=ubuf[t % 2][:, ds])
                        ins.then_inc(sv, 1)
                        vector.wait_ge(sv, V_XO(t))

                    for t in range(NT):
                        for q in range(NQ):
                            j = t * NQ + q
                            i = j % 2
                            vector.wait_ge(lh[i], 16 * (j // 2 + 1))
                            if j >= 2:
                                vector.wait_ge(sp, P_MM(*divmod(j - 2, NQ)))
                            for c in range(DC):
                                cs = slice(c * TT, (c + 1) * TT)
                                vf(vector.tensor_scalar(out=qf[:],
                                                        in0=hbuf[i][:, cs],
                                                        scalar1=rxb[:, 0:1],
                                                        scalar2=MAGIC,
                                                        op0=ALU.mult,
                                                        op1=ALU.add))
                                vf(vector.tensor_scalar(out=xint[:], in0=qf[:],
                                                        scalar1=MAGIC,
                                                        scalar2=None,
                                                        op0=ALU.subtract))
                                ins = vector.tensor_copy(out=hq[i][:, cs],
                                                         in_=xint[:])
                                if c < DC - 1:
                                    vf(ins)
                            ins.then_inc(sv, 1)
                            vector.wait_ge(sv, V_HQ(t, q))
                        if t >= 1:
                            residual_add(t - 1)
                    residual_add(NT - 1)
                    vector.memset(gdv[:], 0.0).then_inc(bar, 1)

                @blk.tensor
                def _(tensor):
                    tensor = EW(tensor)
                    tensor.wait_ge(bar, NB0)
                    tensor.wait_ge(lfix, 16 * (3 if prequant else 2))
                    if not prequant:
                        tensor.wait_ge(sv, V_WMX)
                        tensor.transpose(out=pmm[0][0:1, 0:128], in_=wmx[:],
                                         identity=eyeg[:]).then_inc(sp, 1)
                    else:
                        tensor.wait_ge(sv, V_ONES)
                        tensor.matmul(out=pmm[0][0:1, 0:1], lhsT=onerg[:, 0:1],
                                      rhs=onerg[0:1, 0:1], start=True, stop=True
                                      ).then_inc(sp, 1)
                    tensor.wait_ge(sv, V_SROW)
                    if prequant:
                        tensor.matmul(out=pmm[0][0:1, 0:1], lhsT=onerg[:, 0:1],
                                      rhs=onerg[0:1, 0:1], start=True, stop=True
                                      ).then_inc(sp, 1)
                    else:
                        tensor.matmul(out=pmm[0][:, 0:2], lhsT=onerg[:],
                                      rhs=srow[:], start=True, stop=True
                                      ).then_inc(sp, 1)
                    for t in range(NT):
                        for q in range(NQ):
                            i = (t * NQ + q) % 2
                            if t == 0:
                                tensor.wait_ge(sv, V_WQ(q * DC + DC - 1))
                            tensor.wait_ge(sv, V_HQ(t, q))
                            if q == 0 and t >= 1:
                                tensor.wait_ge(sa, DC * t)
                            for hh in range(DC):
                                h = q * DC + hh
                                for d in range(DC):
                                    ins = tensor.matmul(
                                        out=pmm[d][:],
                                        lhsT=wq2r[h][:, d * 128:(d + 1) * 128],
                                        rhs=hq[i][:, hh * TT:(hh + 1) * TT],
                                        start=(h == 0), stop=(h == NW - 1))
                            ins.then_inc(sp, 1)

                @blk.scalar
                def _(scalar):
                    scalar = EW(scalar)
                    scalar.wait_ge(bar, NB0)
                    if not prequant:
                        for h in range(NW):
                            scalar.wait_ge(sv, V_WQ(h))
                            scalar.dma_start(out=wq2_t[h * 128:(h + 1) * 128, :],
                                             in_=wq2r[h]).then_inc(swq[h % 2], 16)
                    scalar.wait_ge(sv, V_SCAL)
                    for t in range(NT):
                        scalar.wait_ge(sp, P_T(t))
                        if t >= 2:
                            scalar.wait_ge(sx_[t % 2], 16 * (t // 2))
                        for d in range(DC):
                            ds = slice(d * TT, (d + 1) * TT)
                            scalar.activation(out=xo[t % 2][:, ds], in_=pmm[d][:],
                                              func=AF.Identity,
                                              bias=b2s[:, d:d + 1],
                                              scale=s5b[:, 0:1]).then_inc(sa, 1)
                        scalar.wait_ge(sv, V_XO(t))
                        scalar.dma_start(
                            out=dst_t[drow0:drow0 + D, t * TT:(t + 1) * TT
                                      ].rearrange("(c p) t -> p c t", p=128),
                            in_=xo[t % 2][:].rearrange("p (c t) -> p c t", c=DC),
                        ).then_inc(sx_[t % 2], 16)
                    scalar.activation(out=gda[:], in_=gds[:],
                                      func=AF.Identity).then_inc(bar, 1)
                @blk.gpsimd
                def _(gpsimd):
                    gpsimd = EW(gpsimd)
                    gpsimd.wait_ge(bar, NB0 + 19)
            bump()

        def emit_ssm_a(tag, q):
            NB0 = nbv[0]
            B0 = q * TB
            with ExitStack() as st:
                S = st.enter_context
                xbuf = [S(nc.sbuf_tensor(f"{tag}xb{i}", [128, TB], F32))
                        for i in range(2)]
                sq = [S(nc.sbuf_tensor(f"{tag}sq{i}", [128, TT], F32))
                      for i in range(2)]
                xh = [S(nc.sbuf_tensor(f"{tag}xh{d}", [128, TB], BF16))
                      for d in range(DC)]
                wab = [S(nc.sbuf_tensor(f"{tag}wa{d}", [128, D], BF16))
                       for d in range(DC)]
                wbb = [S(nc.sbuf_tensor(f"{tag}wbc{d}", [128, D], BF16))
                       for d in range(DC)]
                wld = [S(nc.sbuf_tensor(f"{tag}wld{i}", [128, D], F32))
                       for i in range(2)]
                alpha = S(nc.sbuf_tensor(f"{tag}al", [128, TB], F32))
                bu = S(nc.sbuf_tensor(f"{tag}bu", [128, TB], F32))
                hs0 = S(nc.sbuf_tensor(f"{tag}hs0", [128, TB], F32))
                Pb = xbuf[1]
                bac = S(nc.sbuf_tensor(f"{tag}bac", [128, DC], F32))
                bbc = S(nc.sbuf_tensor(f"{tag}bbc", [128, DC], F32))
                hend = S(nc.sbuf_tensor(f"{tag}hend", [128, DC], F32))
                t0 = S(nc.sbuf_tensor(f"{tag}t0", [1, TT], F32))
                t1 = S(nc.sbuf_tensor(f"{tag}t1", [1, TT], F32))
                vr = S(nc.sbuf_tensor(f"{tag}vr", [1, TT], F32))
                lnt = S(nc.sbuf_tensor(f"{tag}lnt", [1, TT], F32))
                mr = S(nc.sbuf_tensor(f"{tag}mr", [1, TT], F32))
                rr = S(nc.sbuf_tensor(f"{tag}rr", [1, TT], F32))
                tmpq = S(nc.sbuf_tensor(f"{tag}tmpq", [128, TT], F32))
                psS = [S(nc.psum_tensor(f"{tag}psS{i}", [128, TT], F32))
                       for i in range(2)]
                psM = [S(nc.psum_tensor(f"{tag}psM{i}", [128, TT], F32))
                       for i in range(2)]
                psA = [S(nc.psum_tensor(f"{tag}psA{i}", [128, TT], F32))
                       for i in range(2)]
                psB = [S(nc.psum_tensor(f"{tag}psB{i}", [128, TT], F32))
                       for i in range(2)]
                pA = [psA[0], psA[1], psS[0], psS[1]]
                pB = [psB[0], psB[1], psM[0], psM[1]]
                lfix = POOL[0]
                lx2 = [POOL[1], POOL[2]]
                gw = [GPOOL[0], GPOOL[1]]
                sh = [POOL[5], POOL[6]]
                srl = POOL[7]
                she = POOL[8]
                sv = POOL[9]
                sp = POOL[10]
                sa = POOL[11]
                vf_s = POOL[12]
                af_s = POOL[13]

                V_INIT = 1
                V_WB = lambda k: 2 + k
                V_ROW = lambda t: 18 + 3 * t
                V_VAR = lambda t: 19 + 3 * t
                V_XH = lambda t: 20 + 3 * t
                V_PHA = V_XH(NT - 1)
                V_Z = V_PHA + 1
                V_SC = lambda c: V_Z + 1 + 3 * c + 0
                V_PS = lambda c: V_Z + 2 + 3 * c
                V_HE = lambda c: V_Z + 3 + 3 * c
                V_FIN = V_HE(DC - 1)
                A_SQ = lambda t, d: 9 * t + d + 1
                A_RST = lambda t: 9 * t + 9
                A_EV = lambda c, tt: 9 * NT + 2 * (c * NT + tt) + 2
                P_S1 = lambda t: 10 * t + 1
                P_SQ = lambda t, d: 10 * t + 2 + d
                P_S2 = lambda t: 10 * t + 9
                P_BC = lambda t: 10 * t + 10
                P_AB = lambda c, tt: 10 * NT + c * NT + tt + 1

                @blk.sync
                def _(sync):
                    sync = EW(sync)
                    sync.wait_ge(bar, NB0)
                    sync.dma_start(out=bac[:], in_=bac_t).then_inc(lfix, 16)
                    sync.dma_start(out=bbc[:], in_=bbc_t).then_inc(lfix, 16)
                    for t in range(NT):
                        if t >= 2:
                            sync.wait_ge(sv, V_XH(t - 2))
                        sync.dma_start(
                            out=xbuf[t % 2][:].rearrange("p (c t) -> p c t", c=DC),
                            in_=x2_t[q * D:(q + 1) * D,
                                     t * TT:(t + 1) * TT].rearrange(
                                "(c p) t -> p c t", p=128),
                        ).then_inc(lx2[t % 2], 16)
                    sync.wait_ge(sv, V_FIN)
                    if q % 2 == 0:
                        sync.dma_start(out=hendd_t, in_=hend[:]).then_inc(she, 16)
                        sync.wait_ge(she, 16)
                    sync.wait_ge(srl, 16 * 2 * NT)
                    sync.wait_ge(sh[0], 16 * DC)
                    sync.wait_ge(sh[1], 16 * DC)
                    sync.dma_start(out=gbar_t, in_=gds[:]).then_inc(bar, 16)

                @blk.gpsimd
                def _(gpsimd):
                    gpsimd = EW(gpsimd)
                    gpsimd.wait_ge(bar, NB0)
                    for k in range(2 * DC):
                        if k >= 2:
                            gpsimd.wait_ge(sv, V_WB(k - 2))
                        src_tw = waT_t if k < DC else wbT_t
                        d = k % DC
                        gpsimd.dma_start(out=wld[k % 2][:],
                                         in_=src_tw[d * 128:(d + 1) * 128, :]
                                         ).then_inc(gw[k % 2], 16)
                    gpsimd.memset(gdg[:], 0.0).then_inc(bar, 1)

                @blk.vector
                def _(vector):
                    vector = EW(vector)
                    vector.wait_ge(bar, NB0)
                    vf = FE(vector, vf_s)
                    vector.memset(hend[:], 0.0)
                    vector.memset(t0[:], 0.0).then_inc(sv, 1)
                    vector.wait_ge(sv, V_INIT)
                    for k in range(2 * DC):
                        vector.wait_ge(gw[k % 2], 16 * (k // 2 + 1))
                        dst = wab[k] if k < DC else wbb[k - DC]
                        vector.tensor_copy(out=dst[:], in_=wld[k % 2][:]
                                           ).then_inc(sv, 1)
                    for t in range(NT):
                        ts_ = slice(t * TT, (t + 1) * TT)
                        vector.wait_ge(sp, P_S1(t))
                        if t >= 1:
                            vector.wait_ge(srl, 32 * t)
                        vf(vector.tensor_scalar(out=mr[:], in0=psS[t % 2][0:1, :],
                                                scalar1=INV_D, scalar2=None,
                                                op0=ALU.mult))
                        vector.wait_ge(sp, P_S2(t))
                        vector.tensor_scalar(out=t0[:], in0=psM[t % 2][0:1, :],
                                             scalar1=INV_D, scalar2=None,
                                             op0=ALU.mult).then_inc(sv, 1)
                        vf(vector.tensor_tensor(out=t1[:], in0=mr[:], in1=mr[:],
                                                op=ALU.mult))
                        vector.scalar_tensor_tensor(out=vr[:], in0=t0[:],
                                                    scalar=EPS, in1=t1[:],
                                                    op0=ALU.add, op1=ALU.subtract
                                                    ).then_inc(sv, 1)
                        vector.wait_ge(sv, V_VAR(t))
                        vector.wait_ge(sp, P_BC(t))
                        for d in range(DC):
                            ds = slice(d * TT, (d + 1) * TT)
                            vf(vector.tensor_sub(out=tmpq[:],
                                                 in0=xbuf[t % 2][:, ds],
                                                 in1=psA[t % 2][:]))
                            ins = vector.tensor_tensor(out=xh[d][:, ts_],
                                                       in0=tmpq[:],
                                                       in1=psB[t % 2][:],
                                                       op=ALU.mult)
                            if d < DC - 1:
                                vf(ins)
                        ins.then_inc(sv, 1)
                        vector.wait_ge(sv, V_XH(t))
                    vector.memset(xbuf[0][:], 0.0).then_inc(sv, 1)
                    vector.wait_ge(sv, V_Z)
                    for c in range(DC):
                        vector.wait_ge(sa, A_EV(c, NT - 1))
                        if c >= 1:
                            vector.wait_ge(sh[0], 16 * c)
                            vector.wait_ge(sh[1], 16 * c)
                        vector.tensor_tensor_scan(out=hs0[:], data0=alpha[:],
                                                  data1=bu[:], initial=0.0,
                                                  op0=ALU.mult, op1=ALU.add
                                                  ).then_inc(sv, 1)
                        vector.tensor_tensor_scan(out=Pb[:], data0=alpha[:],
                                                  data1=xbuf[0][:], initial=1.0,
                                                  op0=ALU.mult, op1=ALU.add
                                                  ).then_inc(sv, 1)
                        vector.wait_ge(sv, V_SC(c))
                        vector.tensor_copy(out=hend[:, c:c + 1],
                                           in_=hs0[:, TB - 1:TB]
                                           ).then_inc(sv, 1)
                        vector.wait_ge(sv, V_HE(c))
                    vector.memset(gdv[:], 0.0).then_inc(bar, 1)

                @blk.tensor
                def _(tensor):
                    tensor = EW(tensor)
                    tensor.wait_ge(bar, NB0)
                    tensor.wait_ge(sv, V_INIT)
                    for t in range(NT):
                        ts_ = slice(t * TT, (t + 1) * TT)
                        tensor.wait_ge(lx2[t % 2], 16 * (t // 2 + 1))
                        if t >= 2:
                            tensor.wait_ge(sv, V_XH(t - 2))
                        for d in range(DC):
                            ds = slice(d * TT, (d + 1) * TT)
                            ins = tensor.matmul(out=psS[t % 2][0:1, :],
                                                lhsT=onecg[:],
                                                rhs=xbuf[t % 2][:, ds],
                                                start=(d == 0),
                                                stop=(d == DC - 1))
                        ins.then_inc(sp, 1)
                        for d in range(DC):
                            tensor.wait_ge(sa, A_SQ(t, d))
                            tensor.matmul(out=psM[t % 2][0:1, :], lhsT=onecg[:],
                                          rhs=sq[d % 2][:], start=(d == 0),
                                          stop=(d == DC - 1)).then_inc(sp, 1)
                        tensor.wait_ge(sa, A_RST(t))
                        tensor.matmul(out=psA[t % 2][:], lhsT=onerg[:], rhs=mr[:],
                                      start=True, stop=True)
                        tensor.matmul(out=psB[t % 2][:], lhsT=onerg[:], rhs=rr[:],
                                      start=True, stop=True).then_inc(sp, 1)
                    tensor.wait_ge(sv, V_WB(2 * DC - 1))
                    tensor.wait_ge(sv, V_PHA)
                    for c in range(DC):
                        ns = slice(c * 128, (c + 1) * 128)
                        for tt in range(NT):
                            ts_ = slice(tt * TT, (tt + 1) * TT)
                            j = c * NT + tt
                            if j >= 4:
                                tensor.wait_ge(sa, A_EV(*divmod(j - 4, NT)))
                            for d in range(DC):
                                tensor.matmul(out=pA[j % 4][:], lhsT=wab[d][:, ns],
                                              rhs=xh[d][:, ts_], start=(d == 0),
                                              stop=(d == DC - 1))
                            for d in range(DC):
                                ins = tensor.matmul(out=pB[j % 4][:],
                                                    lhsT=wbb[d][:, ns],
                                                    rhs=xh[d][:, ts_],
                                                    start=(d == 0),
                                                    stop=(d == DC - 1))
                            ins.then_inc(sp, 1)

                @blk.scalar
                def _(scalar):
                    scalar = EW(scalar)
                    scalar.wait_ge(bar, NB0)
                    af = FE(scalar, af_s)
                    scalar.wait_ge(lfix, 32)
                    for t in range(NT):
                        ts_ = slice(t * TT, (t + 1) * TT)
                        scalar.wait_ge(lx2[t % 2], 16 * (t // 2 + 1))
                        if t >= 1:
                            scalar.wait_ge(srl, 32 * t)
                        for d in range(DC):
                            ds = slice(d * TT, (d + 1) * TT)
                            g = t * DC + d
                            if g >= 2:
                                scalar.wait_ge(sp, P_SQ((g - 2) // DC,
                                                        (g - 2) % DC))
                            scalar.activation(out=sq[d % 2][:],
                                              in_=xbuf[t % 2][:, ds],
                                              func=AF.Square).then_inc(sa, 1)
                        scalar.wait_ge(sv, V_VAR(t))
                        if t >= 1:
                            scalar.wait_ge(sa, A_RST(t - 1))
                        af(scalar.activation(out=lnt[:], in_=vr[:], func=AF.Ln))
                        scalar.activation(out=rr[:], in_=lnt[:], func=AF.Exp,
                                          scale=-0.5).then_inc(sa, 1)
                        scalar.wait_ge(sa, A_RST(t))
                        scalar.dma_start(out=stats2_t[0:1, B0 + t * TT:
                                                      B0 + (t + 1) * TT],
                                         in_=mr[:]).then_inc(srl, 16)
                        scalar.dma_start(out=stats2_t[1:2, B0 + t * TT:
                                                      B0 + (t + 1) * TT],
                                         in_=rr[:]).then_inc(srl, 16)
                    for c in range(DC):
                        if c >= 1:
                            scalar.wait_ge(sv, V_SC(c - 1))
                            scalar.dma_start(out=hs0_t[(c - 1) * 128:c * 128, :],
                                             in_=hs0[:]).then_inc(sh[0], 16)
                            scalar.wait_ge(sv, V_PS(c - 1))
                            scalar.dma_start(out=P_t[(c - 1) * 128:c * 128, :],
                                             in_=Pb[:]).then_inc(sh[1], 16)
                        for tt in range(NT):
                            ts_ = slice(tt * TT, (tt + 1) * TT)
                            j = c * NT + tt
                            scalar.wait_ge(sp, P_AB(c, tt))
                            if tt == 0 and c >= 1:
                                scalar.wait_ge(sv, V_PS(c - 1))
                            scalar.activation(out=alpha[:, ts_], in_=pA[j % 4][:],
                                              func=AF.Sigmoid,
                                              bias=bac[:, c:c + 1]
                                              ).then_inc(sa, 1)
                            scalar.activation(out=bu[:, ts_], in_=pB[j % 4][:],
                                              func=AF.Identity,
                                              bias=bbc[:, c:c + 1]
                                              ).then_inc(sa, 1)
                    scalar.wait_ge(sv, V_SC(DC - 1))
                    scalar.dma_start(out=hs0_t[(DC - 1) * 128:DC * 128, :],
                                     in_=hs0[:]).then_inc(sh[0], 16)
                    scalar.wait_ge(sv, V_PS(DC - 1))
                    scalar.dma_start(out=P_t[(DC - 1) * 128:DC * 128, :],
                                     in_=Pb[:]).then_inc(sh[1], 16)
                    scalar.activation(out=gda[:], in_=gds[:],
                                      func=AF.Identity).then_inc(bar, 1)
                @blk.gpsimd
                def _(gpsimd):
                    gpsimd = EW(gpsimd)
                    gpsimd.wait_ge(bar, NB0 + 19)
            bump()

        def emit_ssm_b(tag, q):
            NB0 = nbv[0]
            B0 = q * TB
            hin_src = zhin_t if q % 2 == 0 else hendd_t
            TT2 = 256
            NT2 = TB // TT2
            HT = TB // 2
            with ExitStack() as st:
                S = st.enter_context
                mrl = [S(nc.sbuf_tensor(f"{tag}mrl{i}", [1, TT2], F32))
                       for i in range(2)]
                rrl = [S(nc.sbuf_tensor(f"{tag}rrl{i}", [1, TT2], F32))
                       for i in range(2)]
                hsld = [S(nc.sbuf_tensor(f"{tag}hsl{i}", [128, HT], F32))
                        for i in range(2)]
                pld = [S(nc.sbuf_tensor(f"{tag}pld{i}", [128, HT], F32))
                       for i in range(2)]
                hsb = [S(nc.sbuf_tensor(f"{tag}hsb{c}", [128, TB], BF16))
                       for c in range(DC)]
                ctb = [S(nc.sbuf_tensor(f"{tag}ctb{c}", [128, D], BF16))
                       for c in range(DC)]
                cld = S(nc.sbuf_tensor(f"{tag}cld", [128, D], F32))
                x2b = [S(nc.sbuf_tensor(f"{tag}x2b{i}", [128, DC * TT2], F32))
                       for i in range(2)]
                ystg = [S(nc.sbuf_tensor(f"{tag}ys{i}", [128, DC * TT2], F32))
                        for i in range(2)]
                xhst = S(nc.sbuf_tensor(f"{tag}xhst", [128, DC * TT2], F32))
                hin = S(nc.sbuf_tensor(f"{tag}hin", [128, DC], F32))
                Dcb = S(nc.sbuf_tensor(f"{tag}Dc", [128, DC], F32))
                pbank = [S(nc.psum_tensor(f"{tag}pb{i}", [128, 2 * TT2], F32))
                         for i in range(4)]
                pmm = [pbank[i // 2][:, (i % 2) * TT2:(i % 2 + 1) * TT2]
                       for i in range(DC)]
                psmr = [S(nc.psum_tensor(f"{tag}pr{i}", [128, 2 * TT2], F32))
                        for i in range(2)]
                psM = [psmr[i][:, 0:TT2] for i in range(2)]
                psR = [psmr[i][:, TT2:2 * TT2] for i in range(2)]
                lfix = POOL[0]
                lx = [POOL[1], POOL[2]]
                lm = [POOL[3], POOL[4]]
                gc = [GPOOL[0], GPOOL[1]]
                glh2 = [GPOOL[2], GPOOL[3]]
                glp2 = [GPOOL[4], GPOOL[5]]
                sx_ = [POOL[11], POOL[12]]
                sv = POOL[13]
                sp = POOL[14]
                sa = POOL[15]
                vf_s = POOL[16]

                V_INIT = 1
                V_CB = lambda k: 2 + k
                V_HSB2 = lambda k: 10 + k
                V_HSB = lambda c: V_HSB2(2 * c + 1)
                V_X3 = lambda tt: 26 + tt
                V_FIN = V_X3(NT2 - 1)
                P_Y = lambda tt: tt + 1
                A_EV = lambda tt: 8 * (tt + 1)

                @blk.sync
                def _(sync):
                    sync = EW(sync)
                    sync.wait_ge(bar, NB0)
                    sync.dma_start(out=hin[:], in_=hin_src).then_inc(lfix, 16)
                    sync.dma_start(out=Dcb[:], in_=Dc_t).then_inc(lfix, 16)
                    for tt in range(NT2):
                        ts_ = slice(B0 + tt * TT2, B0 + (tt + 1) * TT2)
                        tsl = slice(tt * TT2, (tt + 1) * TT2)
                        if tt >= 2:
                            sync.wait_ge(sv, V_X3(tt - 2))
                            sync.wait_ge(sx_[tt % 2], 16 * (tt // 2))
                        sync.dma_start(
                            out=x2b[tt % 2][:].rearrange("p (c t) -> p c t", c=DC),
                            in_=x2_t[q * D:(q + 1) * D, tsl].rearrange(
                                "(c p) t -> p c t", p=128),
                        ).then_inc(lx[tt % 2], 16)
                        sync.dma_start(out=mrl[tt % 2][:], in_=stats2_t[0:1, ts_]
                                       ).then_inc(lm[tt % 2], 16)
                        sync.dma_start(out=rrl[tt % 2][:], in_=stats2_t[1:2, ts_]
                                       ).then_inc(lm[tt % 2], 16)
                    sync.wait_ge(sv, V_FIN)
                    sync.wait_ge(sx_[0], 16 * (NT2 // 2))
                    sync.wait_ge(sx_[1], 16 * (NT2 // 2))
                    sync.dma_start(out=gbar_t, in_=gds[:]).then_inc(bar, 16)

                @blk.gpsimd
                def _(gpsimd):
                    gpsimd = EW(gpsimd)
                    gpsimd.wait_ge(bar, NB0)
                    for k in range(DC):
                        if k >= 1:
                            gpsimd.wait_ge(sv, V_CB(k - 1))
                        gpsimd.dma_start(out=cld[:],
                                         in_=cT_t[k * 128:(k + 1) * 128, :]
                                         ).then_inc(gc[k % 2], 16)
                    for k in range(2 * DC):
                        c, half = divmod(k, 2)
                        if k >= 2:
                            gpsimd.wait_ge(sv, V_HSB2(k - 2))
                        hsl = slice(half * HT, (half + 1) * HT)
                        gpsimd.dma_start(out=hsld[k % 2][:],
                                         in_=hs0_t[c * 128:(c + 1) * 128, hsl]
                                         ).then_inc(glh2[k % 2], 16)
                        gpsimd.dma_start(out=pld[k % 2][:],
                                         in_=P_t[c * 128:(c + 1) * 128, hsl]
                                         ).then_inc(glp2[k % 2], 16)
                    gpsimd.memset(gdg[:], 0.0).then_inc(bar, 1)

                @blk.vector
                def _(vector):
                    vector = EW(vector)
                    vector.wait_ge(bar, NB0)
                    vf = FE(vector, vf_s)
                    vector.memset(xhst[:], 0.0).then_inc(sv, 1)
                    vector.wait_ge(sv, V_INIT)
                    vector.wait_ge(lfix, 32)
                    for k in range(DC):
                        vector.wait_ge(gc[k % 2], 16 * (k // 2 + 1))
                        vector.tensor_copy(out=ctb[k][:], in_=cld[:]
                                           ).then_inc(sv, 1)
                    for k in range(2 * DC):
                        c, half = divmod(k, 2)
                        hsl = slice(half * HT, (half + 1) * HT)
                        vector.wait_ge(glh2[k % 2], 16 * (k // 2 + 1))
                        vector.wait_ge(glp2[k % 2], 16 * (k // 2 + 1))
                        vector.scalar_tensor_tensor(out=hsb[c][:, hsl],
                                                    in0=pld[k % 2][:],
                                                    scalar=hin[:, c:c + 1],
                                                    in1=hsld[k % 2][:],
                                                    op0=ALU.mult, op1=ALU.add
                                                    ).then_inc(sv, 1)
                        vector.wait_ge(sv, V_HSB2(k))
                    for tt in range(NT2):
                        vector.wait_ge(lx[tt % 2], 16 * (tt // 2 + 1))
                        vector.wait_ge(sp, P_Y(tt))
                        vector.wait_ge(sa, A_EV(tt))
                        for d in range(DC):
                            ds = slice(d * TT2, (d + 1) * TT2)
                            ins = vector.tensor_sub(out=xhst[:, ds],
                                                    in0=x2b[tt % 2][:, ds],
                                                    in1=psM[tt % 2])
                        vf(ins)
                        for d in range(DC):
                            ds = slice(d * TT2, (d + 1) * TT2)
                            ins = vector.tensor_tensor(out=xhst[:, ds],
                                                       in0=xhst[:, ds],
                                                       in1=psR[tt % 2],
                                                       op=ALU.mult)
                        vf(ins)
                        for d in range(DC):
                            ds = slice(d * TT2, (d + 1) * TT2)
                            ins = vector.scalar_tensor_tensor(
                                out=ystg[tt % 2][:, ds], in0=xhst[:, ds],
                                scalar=Dcb[:, d:d + 1], in1=ystg[tt % 2][:, ds],
                                op0=ALU.mult, op1=ALU.add)
                        vf(ins)
                        for d in range(DC):
                            ds = slice(d * TT2, (d + 1) * TT2)
                            ins = vector.tensor_add(out=x2b[tt % 2][:, ds],
                                                    in0=ystg[tt % 2][:, ds],
                                                    in1=x2b[tt % 2][:, ds])
                        ins.then_inc(sv, 1)
                        vector.wait_ge(sv, V_X3(tt))
                    vector.memset(gdv[:], 0.0).then_inc(bar, 1)

                @blk.tensor
                def _(tensor):
                    tensor = EW(tensor)
                    tensor.wait_ge(bar, NB0)
                    tensor.wait_ge(lfix, 32)
                    tensor.wait_ge(sv, V_HSB(DC - 1))
                    for tt in range(NT2):
                        ts_ = slice(tt * TT2, (tt + 1) * TT2)
                        if tt >= 1:
                            tensor.wait_ge(sa, A_EV(tt - 1))
                        if tt >= 2:
                            tensor.wait_ge(sv, V_X3(tt - 2))
                        tensor.wait_ge(lm[tt % 2], 16 * (2 * (tt // 2) + 2))
                        for d in range(DC):
                            for c in range(DC):
                                tensor.matmul(out=pmm[d],
                                              lhsT=ctb[c][:, d * 128:(d + 1) * 128],
                                              rhs=hsb[c][:, ts_], start=(c == 0),
                                              stop=(c == DC - 1))
                        tensor.matmul(out=psM[tt % 2], lhsT=onerg[:],
                                      rhs=mrl[tt % 2][:], start=True, stop=True)
                        tensor.matmul(out=psR[tt % 2], lhsT=onerg[:],
                                      rhs=rrl[tt % 2][:], start=True, stop=True
                                      ).then_inc(sp, 1)

                @blk.scalar
                def _(scalar):
                    scalar = EW(scalar)
                    scalar.wait_ge(bar, NB0)
                    for tt in range(NT2):
                        scalar.wait_ge(sp, P_Y(tt))
                        if tt >= 2:
                            scalar.wait_ge(sv, V_X3(tt - 2))
                        for d in range(DC):
                            ds = slice(d * TT2, (d + 1) * TT2)
                            scalar.activation(out=ystg[tt % 2][:, ds], in_=pmm[d],
                                              func=AF.Identity).then_inc(sa, 1)
                        scalar.wait_ge(sv, V_X3(tt))
                        scalar.dma_start(
                            out=x3_t[q * D:(q + 1) * D,
                                     tt * TT2:(tt + 1) * TT2
                                     ].rearrange("(c p) t -> p c t", p=128),
                            in_=x2b[tt % 2][:].rearrange("p (c t) -> p c t", c=DC),
                        ).then_inc(sx_[tt % 2], 16)
                    scalar.activation(out=gda[:], in_=gds[:],
                                      func=AF.Identity).then_inc(bar, 1)
                @blk.gpsimd
                def _(gpsimd):
                    gpsimd = EW(gpsimd)
                    gpsimd.wait_ge(bar, NB0 + 19)
            bump()

        def emit_own_copy(tag, pairs):
            NB0 = nbv[0]
            with ExitStack() as st:
                S = st.enter_context
                cp = GPOOL[0]

                @blk.gpsimd
                def _(gpsimd):
                    gpsimd = EW(gpsimd)
                    gpsimd.wait_ge(bar, NB0)
                    n = 0
                    for dst_t, src_t, nrows, okey in pairs:
                        n += 16
                        gpsimd.dma_start(
                            out=dst_t[0:nrows, :],
                            in_=src_t[bass.ds(ids[okey], nrows), :],
                        ).then_inc(cp, 16)
                        gpsimd.wait_ge(cp, n)
                    gpsimd.memset(gdg[:], 0.0).then_inc(bar, 1)

                @blk.vector
                def _(vector):
                    vector = EW(vector)
                    vector.wait_ge(bar, NB0)
                    vector.memset(gdv[:], 0.0).then_inc(bar, 1)

                @blk.scalar
                def _(scalar):
                    scalar = EW(scalar)
                    scalar.wait_ge(bar, NB0)
                    scalar.activation(out=gda[:], in_=gds[:],
                                      func=AF.Identity).then_inc(bar, 1)

                @blk.sync
                def _(sync):
                    sync = EW(sync)
                    sync.wait_ge(bar, NB0)
                    sync.dma_start(out=gbar_t, in_=gds[:]).then_inc(bar, 16)
                @blk.gpsimd
                def _(gpsimd):
                    gpsimd = EW(gpsimd)
                    gpsimd.wait_ge(bar, NB0 + 19)
            bump()

        emit_stats("p1", lambda t: u_t[:, t * TT:(t + 1) * TT],
                   stats1_t, mloc_t[0])
        emit_combine(0)
        for b in range(NBK):
            emit_ffn_in(f"p2b{b}", b,
                        (lambda b2: lambda t: u_t[:, b2 * TB + t * TT:
                                                  b2 * TB + (t + 1) * TT])(b),
                        stats1_t, mall_t[0], h1_t,
                        prequant=(b > 0), zero_gmx=(b == 0), final=(b == NBK - 1),
                        mloc_out=mloc_t[1])
        emit_combine(1)
        for b in range(NBK):
            emit_ffn_out(f"p3b{b}", b, u_t, h1_t, mall_t[1], x2_t,
                         prequant=(b > 0), drow0=b * D)
        for q in range(NBK):
            emit_ssm_a(f"sa{q}", q)
            emit_ssm_b(f"sb{q}", q)
        emit_stats("p4",
                   lambda t: x3_t[(t // NT) * D:(t // NT + 1) * D,
                                  (t % NT) * TT:(t % NT + 1) * TT],
                   stats3_t, mloc_t[2])
        emit_combine(2)
        for b in range(NBK):
            emit_ffn_in(f"p5b{b}", b,
                        (lambda b2: lambda t: x3_t[b2 * D:(b2 + 1) * D,
                                                   t * TT:(t + 1) * TT])(b),
                        stats3_t, mall_t[2], h1_t,
                        prequant=True, zero_gmx=(b == 0), final=(b == NBK - 1),
                        mloc_out=mloc_t[3])
        emit_combine(3, copies=[(x3own_t, x3_t, D, "offd"),
                                (h1own_t, h1_t, H, "offh")])
        emit_ffn_out("p6", 0, x3own_t, h1own_t, mall_t[3], xo_t,
                     prequant=True, hrow0=0, drow0=0)

    return nc



def _asc(a):
    return np.ascontiguousarray(a)


_EYE = np.eye(128, dtype=np.float32)


def _bcol(v):
    return np.ascontiguousarray(np.asarray(v).reshape(-1, 128).T)


def kernel(**inputs):
    u = np.asarray(inputs["L0_attn_u"])
    w1T = _asc(np.asarray(inputs["L0_ffn_w1"]).T)
    w2T = _asc(np.asarray(inputs["L0_ffn_w2"]).T)
    waT = _asc(np.asarray(inputs["L0_attn_W_alpha_w"]).T)
    wbT = _asc(np.asarray(inputs["L0_attn_W_B_w"]).T)
    cT = _asc(np.asarray(inputs["L0_attn_C_w"]).T)
    b1c = _bcol(inputs["L0_ffn_b1"])
    b2c = _bcol(inputs["L0_ffn_b2"])
    bac = _bcol(inputs["L0_attn_W_alpha_b"])
    bbc = _bcol(inputs["L0_attn_W_B_b"])
    Dcc = _bcol(inputs["L0_attn_D"])
    zhin = np.zeros((128, DC), np.float32)
    cores = list(range(NCORE))

    nc = build_mega()
    in_maps = []
    for c in cores:
        pb = (0, 1) if c % 2 == 0 else (2, 3)
        uc = _asc(np.concatenate([u[pb[0]].T, u[pb[1]].T], axis=1))
        own_off = (c // 2) * TB
        in_maps.append({
            "u": uc, "w1T": w1T, "w2T": w2T, "waT": waT, "wbT": wbT,
            "cT": cT, "b1c": b1c, "b2c": b2c, "bac": bac, "bbc": bbc,
            "Dc": Dcc, "eye": _EYE,
            "slot": np.array([[c % 2, 1 - c % 2]], np.int32),
            "zhin": zhin,
            "ooff": np.array([[(c // 2) * D, (c // 2) * H]], np.int32),
        })
    r = run_bass_kernel_spmd(nc, in_maps, core_ids=cores)

    out = np.empty((4, 8192, D), np.float32)
    for c in cores:
        pb = (0, 1) if c % 2 == 0 else (2, 3)
        own_off = (c // 2) * TB
        batch = pb[own_off // 8192]
        toff = own_off % 8192
        out[batch, toff:toff + TB, :] = np.asarray(r.results[c]["xo"]).T
    return out
